# version 05
# baseline (speedup 1.0000x reference)
import contextlib
import numpy as np
import ml_dtypes

import concourse.bass as bass
import concourse.mybir as mybir
from concourse.bass_utils import run_bass_kernel_spmd

F32 = mybir.dt.float32
BF16 = mybir.dt.bfloat16
AX = mybir.AxisListType
OP = mybir.AluOpType
ACT = mybir.ActivationFunctionType
BF = ml_dtypes.bfloat16

D = 1024
N = 2048
H = 8
E = 64
FF = 2816
NT = 16
KC = 8
FKC = 22
EPS_RMS = float(np.finfo(np.float32).eps)
NEG_BIG = -float(np.finfo(np.float32).max)

THR_U = [0.25, 0.75, 1.25, 1.75]
THR_V = [2.5, 3.5, 5.0, 5.0]


class Sem:
    def __init__(self, stack, nc, name):
        self.h = stack.enter_context(nc.semaphore(name))
        self.n = 0

    def inc(self, instr, k=1):
        instr.then_inc(self.h, k)
        self.n += k
        return self.n


def fence(eng, sem, delay=384):
    return sem.inc(eng.drain())


def emit_quant(v, X, out, q, n, nb):
    aa = q["aa"][:, :n]
    xn = q["xn"][:, :n]
    sg = q["sg"][:, :n]
    vv = q["vv"][:, :n]
    rex = q["rex"][:, :n]
    tr = q["tr"]
    bmax = q["bmax"][:, :nb]
    scale = q["scale"][:, :nb]
    rs = q["rs"][:, :nb]
    n2, n4, n8 = n // 2, n // 4, n // 8

    def s2(ap, w, par):
        return ap[:, par:par + 2 * w - 1:2]

    v.scalar_tensor_tensor(aa, X, -1.0, X, op0=OP.mult, op1=OP.max)
    t1 = tr[:, 0:n2]
    t2 = tr[:, n2:n2 + n4]
    t3 = tr[:, n2 + n4:n2 + n4 + n8]
    v.tensor_tensor(t1, s2(aa, n2, 0), s2(aa, n2, 1), op=OP.max)
    v.tensor_tensor(t2, s2(t1, n4, 0), s2(t1, n4, 1), op=OP.max)
    v.tensor_tensor(t3, s2(t2, n8, 0), s2(t2, n8, 1), op=OP.max)
    v.tensor_tensor(bmax, s2(t3, nb, 0), s2(t3, nb, 1), op=OP.max)
    v.tensor_scalar(sg, X, 0.0, 2.0, op0=OP.is_ge, op1=OP.mult)
    v.tensor_scalar(scale, bmax, 1.0 / 6.0, 1e-12, op0=OP.mult, op1=OP.max)
    v.tensor_scalar(sg, sg, 1.0, None, op0=OP.subtract)
    v.reciprocal(rs, scale)
    v.drain()
    v.tensor_copy(s2(t3, nb, 0), rs)
    v.tensor_copy(s2(t3, nb, 1), rs)
    v.drain()
    v.tensor_copy(s2(t2, n8, 0), t3)
    v.tensor_copy(s2(t2, n8, 1), t3)
    v.drain()
    v.tensor_copy(s2(t1, n4, 0), t2)
    v.tensor_copy(s2(t1, n4, 1), t2)
    v.drain()
    v.tensor_copy(s2(rex, n2, 0), t1)
    v.tensor_copy(s2(rex, n2, 1), t1)
    v.drain()
    v.tensor_tensor(xn, aa, rex, op=OP.mult)
    v.tensor_scalar(aa, xn, THR_U[0], None, op0=OP.is_gt)
    for t in THR_U[1:]:
        v.scalar_tensor_tensor(aa, xn, t, aa, op0=OP.is_gt, op1=OP.add)
    v.tensor_scalar(vv, xn, THR_V[0], None, op0=OP.is_gt)
    for t in THR_V[1:]:
        v.scalar_tensor_tensor(vv, xn, t, vv, op0=OP.is_gt, op1=OP.add)
    v.scalar_tensor_tensor(aa, aa, 0.5, vv, op0=OP.mult, op1=OP.add)
    v.tensor_tensor(aa, aa, sg, op=OP.mult)
    v.drain()
    v.tensor_copy(s2(t3, nb, 0), scale)
    v.tensor_copy(s2(t3, nb, 1), scale)
    v.drain()
    v.tensor_copy(s2(t2, n8, 0), t3)
    v.tensor_copy(s2(t2, n8, 1), t3)
    v.drain()
    v.tensor_copy(s2(t1, n4, 0), t2)
    v.tensor_copy(s2(t1, n4, 1), t2)
    v.drain()
    v.tensor_copy(s2(rex, n2, 0), t1)
    v.tensor_copy(s2(rex, n2, 1), t1)
    v.drain()
    v.tensor_tensor(out, aa, rex, op=OP.mult)


def build_program(debug=False):
    nc = bass.Bass("TRN2", target_bir_lowering=False,
                   detect_race_conditions=False)
    dbg = {}

    def mk_dbg(name, shape, dt):
        dbg[name] = nc.dram_tensor("dbg_" + name, shape, dt, kind="ExternalOutput")
        return dbg[name]

    x_in = nc.dram_tensor("x", [2, N, D], F32, kind="ExternalInput")
    wq_in = nc.dram_tensor("wq", [H, D, E], F32, kind="ExternalInput")
    wk_in = nc.dram_tensor("wk", [H, D, E], F32, kind="ExternalInput")
    wv_in = nc.dram_tensor("wv", [H, D, E], F32, kind="ExternalInput")
    wo_in = nc.dram_tensor("wo", [H, E, D], F32, kind="ExternalInput")
    wg_in = nc.dram_tensor("wgates", [D, H], F32, kind="ExternalInput")
    rms_in = nc.dram_tensor("rmsw", [D], F32, kind="ExternalInput")
    c_in = nc.dram_tensor("cvec", [2, D], F32, kind="ExternalInput")
    wmod_in = nc.dram_tensor("wmod", [D, 6 * D], F32, kind="ExternalInput")
    wgate_in = nc.dram_tensor("wgate", [FF, D], F32, kind="ExternalInput")
    wup_in = nc.dram_tensor("wup", [FF, D], F32, kind="ExternalInput")
    wdown_in = nc.dram_tensor("wdown", [D, FF], F32, kind="ExternalInput")
    iden_in = nc.dram_tensor("iden", [128, 128], BF16, kind="ExternalInput")
    mask_in = nc.dram_tensor("cmask", [128, 128], F32, kind="ExternalInput")
    cvals_in = nc.dram_tensor("cvals", [128, 8], F32, kind="ExternalInput")
    y_out = nc.dram_tensor("y", [2, N, D], F32, kind="ExternalOutput")

    xv = x_in.rearrange("s n d -> (s n) d")
    yv = y_out.rearrange("s n d -> (s n) d")

    modv = nc.dram_tensor("modv", [12, D], F32)
    x1d = nc.dram_tensor("x1d", [2 * N, D], F32)
    wgq_d = nc.dram_tensor("wgq", [KC, 128, FF], BF16)
    wuq_d = nc.dram_tensor("wuq", [KC, 128, FF], BF16)
    wdq_d = nc.dram_tensor("wdq", [FKC, 128, D], BF16)

    st = contextlib.ExitStack()
    with st:
        def SB(stack, name, shape, dt):
            return stack.enter_context(nc.sbuf_tensor(name, shape, dt))

        def PS(stack, name, shape, dt):
            return stack.enter_context(nc.psum_tensor(name, shape, dt))

        idn = st.enter_context(nc.sbuf_tensor("idn", [128, 128], BF16))
        mn_tiles = [st.enter_context(nc.sbuf_tensor("mnt%d" % g, [128, 1], F32))
                    for g in range(32)]
        rsv_tiles = [st.enter_context(nc.sbuf_tensor("rst%d" % g, [128, 1], F32))
                     for g in range(32)]
        CB1 = st.enter_context(nc.sbuf_tensor("CB1", [128, D], F32))
        SBr = st.enter_context(nc.sbuf_tensor("SBr", [128, D], F32))
        GBr = st.enter_context(nc.sbuf_tensor("GBr", [128, D], F32))
        epsln = st.enter_context(nc.sbuf_tensor("epsln", [128, 1], F32))
        nhalf = st.enter_context(nc.sbuf_tensor("nhalf", [128, 1], F32))
        negone = st.enter_context(nc.sbuf_tensor("negone", [128, 1], F32))
        c0125 = st.enter_context(nc.sbuf_tensor("c0125", [128, 1], F32))
        c1d = st.enter_context(nc.sbuf_tensor("c1d", [128, 1], F32))
        epsrms = st.enter_context(nc.sbuf_tensor("epsrms", [128, 1], F32))
        c025n = st.enter_context(nc.sbuf_tensor("c025n", [128, 1], F32))
        am2 = st.enter_context(nc.sbuf_tensor("am2", [128, 1], F32))
        anm2 = st.enter_context(nc.sbuf_tensor("anm2", [128, 1], F32))
        avrp = st.enter_context(nc.sbuf_tensor("avrp", [128, 1], F32))
        attn_es = contextlib.ExitStack()
        msk = SB(attn_es, "msk", [128, 128], F32)
        wqb = SB(attn_es, "wqb", [128, KC, 512], BF16)
        wkb = SB(attn_es, "wkb", [128, KC, 512], BF16)
        wvb = SB(attn_es, "wvb", [128, KC, 512], BF16)
        wgb = SB(attn_es, "wgb", [128, KC, 8], BF16)
        wob = SB(attn_es, "wob", [128, 4, D], BF16)
        rcols = SB(attn_es, "rcols", [128, KC], F32)
        rca = [SB(attn_es, "rca%d" % kc, [128, 1], F32) for kc in range(KC)]
        CA1 = SB(attn_es, "CA1", [128, D], F32)
        SAr = SB(attn_es, "SAr", [128, D], F32)
        GAr = SB(attn_es, "GAr", [128, D], F32)

        dX = Sem(st, nc, "dX")
        dW = Sem(st, nc, "dW")
        dY = Sem(st, nc, "dY")
        sT = Sem(st, nc, "sT")
        sV = Sem(st, nc, "sV")
        sA = Sem(st, nc, "sA")
        sG = Sem(st, nc, "sG")

        wqv = wq_in.rearrange("h (c p) e -> p c h e", p=128)
        wkv = wk_in.rearrange("h (c p) e -> p c h e", p=128)
        wvv = wv_in.rearrange("h (c p) e -> p c h e", p=128)
        wgv = wg_in.rearrange("(c p) h -> p c h", p=128)
        wov = wo_in.rearrange("h e d -> (h e) d").rearrange("(c p) d -> p c d", p=128)
        rmsv = rms_in.rearrange("(c p) -> p c", p=128)
        cv2 = c_in.rearrange("b (cc p) -> p b cc", p=128)

        with contextlib.ExitStack() as r0:
            csb2 = SB(r0, "csb2", [128, 2, 8], F32)
            e1b = SB(r0, "e1b", [128, 16], F32)
            rrb = SB(r0, "rrb", [128, 16], F32)
            scb2 = SB(r0, "scb2", [128, 2, 8], BF16)
            modT = SB(r0, "modT", [128, 96], F32)
            gsc_a = SB(r0, "gsc_a", [128, 16], F32)
            gsc_s = SB(r0, "gsc_s", [128, 16], F32)
            wmod_sb = SB(r0, "wmod_sb", [128, KC, 6 * D], BF16)
            wraw = SB(r0, "wraw", [128, KC, 512], F32)
            with nc.Block() as b:
                @b.sync
                def _(s):
                    s.dma_start(idn[:, :], iden_in[:, :]).then_inc(dX.h, 16); dX.n += 16
                    s.dma_start(msk[:, :], mask_in[:, :]).then_inc(dX.h, 16); dX.n += 16
                    nc.allow_non_contiguous_dma("tiny strided constant loads")
                    s.dma_start(csb2[:, :, :], cv2, allow_slow_non_contiguous=True
                                ).then_inc(dX.h, 16); dX.n += 16
                    s.dma_start(rcols[:, :], rmsv, allow_slow_non_contiguous=True
                                ).then_inc(dX.h, 16); dX.n += 16
                    for kc in range(KC):
                        s.dma_start(rca[kc][:, :], rmsv[:, kc:kc + 1],
                                    allow_slow_non_contiguous=True
                                    ).then_inc(dX.h, 16); dX.n += 16

                @b.scalar
                def _(a):
                    a.add_instruction(mybir.InstLoadActFuncSet(
                        name=nc.get_next_instruction_name(),
                        act_func_set_id=6, ins=[], outs=[]))

                @b.gpsimd
                def _(g):
                    for kc in range(KC):
                        g.dma_start(wmod_sb[:, kc, :],
                                    wmod_in.rearrange("(c p) n -> p c n", p=128)[:, kc, :]
                                    ).then_inc(dW.h, 16); dW.n += 16
                    for c in range(4):
                        g.dma_start(wob[:, c, :], wov[:, c, :]).then_inc(dW.h, 16); dW.n += 16
                    g.wait_ge(dW.h, dW.n)

            with nc.Block() as b:
                @b.sync
                def _(s):
                    nc.allow_non_contiguous_dma("tiny const loads")
                    for k, tens in enumerate((epsln, nhalf, negone, c0125,
                                              c1d, epsrms, c025n)):
                        s.dma_start(tens[:, :], cvals_in[:, k:k + 1],
                                    allow_slow_non_contiguous=True
                                    ).then_inc(dX.h, 16); dX.n += 16
            base_loads = dX.n

            csf = csb2.rearrange("p b cc -> p (b cc)")
            scf2 = scb2.rearrange("p b cc -> p (b cc)")
            with nc.Block() as b:
                @b.scalar
                def _(a):
                    a.wait_ge(dX.h, base_loads)
                    a.activation(e1b[:, :], csf, ACT.Exp, scale=negone[:, 0:1])
                    a.drain()
                    a.activation(e1b[:, :], e1b[:, :], ACT.Copy, bias=1.0)
                    fence(a, sA)

            with nc.Block() as b:
                @b.vector
                def _(v):
                    v.reciprocal(rrb[:, :], e1b[:, :])
                    fence(v, sV)

            with nc.Block() as b:
                @b.gpsimd
                def _(g):
                    g.tensor_tensor(scf2, csf, rrb[:, :], op=OP.mult)
                    fence(g, sG)

            with contextlib.ExitStack() as r1:
                modp = PS(r1, "modp", [128, 96], F32)
                with nc.Block() as b:
                    @b.tensor
                    def _(t):
                        for n48 in range(48):
                            for kc in range(KC):
                                t.matmul(modp[:, n48 * 2:n48 * 2 + 2],
                                         wmod_sb[:, kc, n48 * 128:(n48 + 1) * 128],
                                         scb2[:, :, kc],
                                         start=(kc == 0), stop=(kc == KC - 1))
                with nc.Block() as b:
                    @b.vector
                    def _(v):
                        v.tensor_copy(modT[:, :], modp[:, :])
                        fence(v, sV)

            with nc.Block() as b:
                @b.vector
                def _(v):
                    v.tensor_scalar(modT[:, 16:32], modT[:, 16:32], 1.0, None,
                                    op0=OP.add)
                    v.tensor_scalar(modT[:, 64:80], modT[:, 64:80], 1.0, None,
                                    op0=OP.add)
                    fence(v, sV)
            for q in (2, 5):
                sl = modT[:, q * 16:(q + 1) * 16]
                with nc.Block() as b:
                    @b.scalar
                    def _(a):
                        a.activation(gsc_a[:, :], sl, ACT.Abs)
                        a.drain()
                        a.activation(gsc_a[:, :], gsc_a[:, :], ACT.Relu,
                                     bias=c025n[:, 0:1])
                        a.drain()
                        a.activation(gsc_a[:, :], gsc_a[:, :], ACT.Copy, bias=0.25)
                        a.activation(gsc_s[:, :], sl, ACT.Sign)
                        fence(a, sA)
                with nc.Block() as b:
                    @b.gpsimd
                    def _(g):
                        g.tensor_tensor(sl, gsc_a[:, :], gsc_s[:, :],
                                        op=OP.mult)
                        fence(g, sG)

            modTv = modT.rearrange("p (x cc b) -> p x cc b", cc=8, b=2)
            modvv = modv.rearrange("r (cc p) -> r p cc", p=128)
            with nc.Block() as b:
                @b.gpsimd
                def _(g):
                    nc.allow_non_contiguous_dma("tiny mod row stores")
                    for q in range(6):
                        for bb in range(2):
                            g.dma_start(modvv[q * 2 + bb, :, :],
                                        modTv[:, q, :, bb],
                                        allow_slow_non_contiguous=True
                                        ).then_inc(dY.h, 16); dY.n += 16
            mod_stored = dY.n

            for (win, wdst, ncols) in ((wqv, wqb, 512), (wkv, wkb, 512),
                                       (wvv, wvb, 512), (wgv, wgb, 8)):
                with nc.Block() as b:
                    @b.sync
                    def _(s):
                        for kc in range(KC):
                            if ncols == 512:
                                dst = wraw[:, kc, :ncols].rearrange(
                                    "p (h e) -> p h e", e=E)
                                srcv = win[:, kc, :, :]
                            else:
                                dst = wraw[:, kc, :ncols]
                                srcv = win[:, kc, :]
                            s.dma_start(dst, srcv).then_inc(dX.h, 16); dX.n += 16
                ldc = dX.n
                is_wq = wdst is wqb
                with nc.Block() as b:
                    @b.vector
                    def _(v):
                        v.wait_ge(dX.h, ldc)
                        for kc in range(KC):
                            if is_wq:
                                v.tensor_scalar(wdst[:, kc, :], wraw[:, kc, :ncols],
                                                rca[kc][:, 0:1], 0.125,
                                                op0=OP.mult, op1=OP.mult)
                            else:
                                v.tensor_scalar(wdst[:, kc, :], wraw[:, kc, :ncols],
                                                rca[kc][:, 0:1], None, op0=OP.mult)

            if debug:
                d_mod = mk_dbg("modT", [128, 96], F32)
                d_rc = mk_dbg("rcols", [128, KC], F32)
                d_wqb = mk_dbg("wqb", [128, 512], BF16)
                d_wvb = mk_dbg("wvb", [128, 512], BF16)
                d_wob = mk_dbg("wob", [128, D], BF16)
                d_sct = mk_dbg("scb2", [128, 2, 8], BF16)
                with nc.Block() as b:
                    @b.gpsimd
                    def _(g):
                        g.dma_start(d_mod[:, :], modT[:, :]).then_inc(dY.h, 16); dY.n += 16
                        g.dma_start(d_rc[:, :], rcols[:, :]).then_inc(dY.h, 16); dY.n += 16
                        g.dma_start(d_wqb[:, :], wqb[:, 0, :]).then_inc(dY.h, 16); dY.n += 16
                        g.dma_start(d_wvb[:, :], wvb[:, 0, :]).then_inc(dY.h, 16); dY.n += 16
                        g.dma_start(d_wob[:, :], wob[:, 0, :]).then_inc(dY.h, 16); dY.n += 16
                        g.dma_start(d_sct[:, :, :], scb2[:, :, :]).then_inc(dY.h, 16); dY.n += 16

        with contextlib.ExitStack() as r2:
            fwin0 = SB(r2, "fwin0", [128, FF], F32)
            fwin1 = SB(r2, "fwin1", [128, FF], F32)
            fq_xn = SB(r2, "fq_xn", [128, FF], F32)
            fq_aa = SB(r2, "fq_aa", [128, FF], F32)
            fq_sg = SB(r2, "fq_sg", [128, FF], F32)
            fq_vv = SB(r2, "fq_vv", [128, FF], F32)
            fq_bm = SB(r2, "fq_bm", [128, 256], F32)
            fq_sc = SB(r2, "fq_sc", [128, 256], F32)
            fq_rs = SB(r2, "fq_rs", [128, 256], F32)
            fq_tr = SB(r2, "fq_tr", [128, FF], F32)
            fq_rex = SB(r2, "fq_rex", [128, FF], F32)
            fqo = SB(r2, "fqo", [128, FF], BF16)
            fevac = SB(r2, "fevac", [128, FF], BF16)
            ftp0 = PS(r2, "ftp0", [128, 1024], BF16)
            ftp1 = PS(r2, "ftp1", [128, 1024], BF16)
            fq = {"xn": fq_xn, "aa": fq_aa, "sg": fq_sg, "vv": fq_vv,
                  "bmax": fq_bm, "scale": fq_sc, "rs": fq_rs,
                  "tr": fq_tr, "rex": fq_rex}
            fbufs = [fwin0, fwin1]
            ftps = [ftp0, ftp1]
            tiles = []
            wgatev = wgate_in.rearrange("(t p) d -> t p d", p=128)
            wupv = wup_in.rearrange("(t p) d -> t p d", p=128)
            wdownv = wdown_in.rearrange("(t p) d -> t p d", p=128)
            for t in range(FKC):
                tiles.append(("wg", t))
            for t in range(FKC):
                tiles.append(("wu", t))
            for t in range(KC):
                tiles.append(("wd", t))

            def tile_src(kind, t):
                return {"wg": wgatev, "wu": wupv, "wd": wdownv}[kind][t, :, :]

            def tile_n(kind):
                return D if kind in ("wg", "wu") else FF

            with nc.Block() as b:
                @b.sync
                def _(s):
                    k0, t0 = tiles[0]
                    s.dma_start(fbufs[0][:, :tile_n(k0)], tile_src(k0, t0)
                                ).then_inc(dX.h, 16); dX.n += 16

            for ti, (kind, t) in enumerate(tiles):
                n = tile_n(kind)
                nb = n // 16
                nch = n // 128
                ld_done = dX.n
                with nc.Block() as b:
                    @b.sync
                    def _(s):
                        if ti + 1 < len(tiles):
                            k1, t1 = tiles[ti + 1]
                            s.dma_start(fbufs[(ti + 1) % 2][:, :tile_n(k1)],
                                        tile_src(k1, t1)).then_inc(dX.h, 16); dX.n += 16

                    @b.vector
                    def _(v):
                        v.wait_ge(dX.h, ld_done)
                        emit_quant(v, fbufs[ti % 2][:, :n], fqo[:, :n], fq, n, nb)
                ngroups = (nch + 7) // 8
                store_done = dY.n
                for gi in range(ngroups):
                    c0 = gi * 8
                    c1 = min(c0 + 8, nch)
                    with nc.Block() as b:
                        @b.tensor
                        def _(tt):
                            for kc in range(c0, c1):
                                tt.transpose(ftps[gi % 2][:, (kc - c0) * 128:(kc - c0 + 1) * 128],
                                             fqo[:, kc * 128:(kc + 1) * 128],
                                             idn[:, :])
                    with nc.Block() as b:
                        @b.vector
                        def _(v):
                            v.wait_ge(dY.h, store_done)
                            v.tensor_copy(fevac[:, c0 * 128:c1 * 128],
                                          ftps[gi % 2][:, :(c1 - c0) * 128])
                with nc.Block() as b:
                    @b.gpsimd
                    def _(g):
                        dst = {"wg": wgq_d, "wu": wuq_d, "wd": wdq_d}[kind]
                        for kc in range(nch):
                            g.dma_start(dst[kc, :, t * 128:(t + 1) * 128],
                                        fevac[:, kc * 128:(kc + 1) * 128]
                                        ).then_inc(dY.h, 16); dY.n += 16
        ffnw_stored = dY.n

        with contextlib.ExitStack() as r3:
            xb0 = SB(r3, "xb0", [128, D], F32)
            xb1 = SB(r3, "xb1", [128, D], F32)
            hbuf0 = SB(r3, "hbuf0", [128, D], F32)
            hbuf1 = SB(r3, "hbuf1", [128, D], F32)
            tbuf0 = SB(r3, "tbuf0", [128, D], BF16)
            tbuf1 = SB(r3, "tbuf1", [128, D], BF16)
            tT0 = SB(r3, "tT0", [128, KC, 128], BF16)
            tT1 = SB(r3, "tT1", [128, KC, 128], BF16)
            sqscr = SB(r3, "sqscr", [128, D], BF16)
            ssum0 = SB(r3, "ssum0", [128, 1], F32)
            ssum1 = SB(r3, "ssum1", [128, 1], F32)
            ssq0 = SB(r3, "ssq0", [128, 1], F32)
            ssq1 = SB(r3, "ssq1", [128, 1], F32)
            mn0 = SB(r3, "mn0", [128, 1], F32)
            mn1 = SB(r3, "mn1", [128, 1], F32)
            vr0 = SB(r3, "vr0", [128, 1], F32)
            vr1 = SB(r3, "vr1", [128, 1], F32)
            rsv0 = SB(r3, "rsv0", [128, 1], F32)
            rsv1 = SB(r3, "rsv1", [128, 1], F32)
            rsq0 = SB(r3, "rsq0", [128, 1], F32)
            rsq1 = SB(r3, "rsq1", [128, 1], F32)
            gtmp = SB(r3, "gtmp", [128, 8], F32)
            qTs = SB(r3, "qTs", [128, 4, N], BF16)
            kTs = SB(r3, "kTs", [128, 4, N], BF16)
            vTs = SB(r3, "vTs", [128, NT, 512], BF16)
            gSig = SB(r3, "gSig", [128, NT, 8], F32)
            outTs = SB(r3, "outTs", [128, 4, N], BF16)
            Psb = SB(r3, "Psb", [128, N], BF16)
            ptpsb = SB(r3, "ptpsb", [128, N], BF16)
            rowmx = SB(r3, "rowmx", [128, 1], F32)
            nbias = SB(r3, "nbias", [128, 1], F32)
            rsum = SB(r3, "rsum", [128, 1], F32)
            rrs = SB(r3, "rrs", [128, 1], F32)
            gam = SB(r3, "gam", [128, 1], F32)
            diag = SB(r3, "diag", [128, 128], BF16)
            x1b0 = SB(r3, "x1b0", [128, D], F32)
            x1b1 = SB(r3, "x1b1", [128, D], F32)
            d_stats_sb = SB(r3, "d_stats_sb", [128, 8], F32)
            xbs = [xb0, xb1]
            hbufs = [hbuf0, hbuf1]
            tbufs = [tbuf0, tbuf1]
            tTs = [tT0, tT1]
            ssums = [ssum0, ssum1]
            ssqs = [ssq0, ssq1]
            mns = [mn0, mn1]
            vrs = [vr0, vr1]
            rsvs = [rsv0, rsv1]
            rsqs = [rsq0, rsq1]
            x1bs = [x1b0, x1b1]

            for seq in range(2):
                tok0 = seq * N

                with nc.Block() as b:
                    @b.gpsimd
                    def _(g):
                        g.wait_ge(dY.h, mod_stored)
                        g.dma_start(SAr[:, :], modv[0 * 2 + seq:0 * 2 + seq + 1, :]
                                    .broadcast_to([128, D])).then_inc(dW.h, 16); dW.n += 16
                        g.dma_start(CA1[:, :], modv[1 * 2 + seq:1 * 2 + seq + 1, :]
                                    .broadcast_to([128, D])).then_inc(dW.h, 16); dW.n += 16
                bc_done = dW.n

                with contextlib.ExitStack() as r4:
                    tps0 = PS(r4, "tps0_%d" % seq, [128, 1024], BF16)
                    tps1 = PS(r4, "tps1_%d" % seq, [128, 1024], BF16)
                    qps = PS(r4, "qps_%d" % seq, [128, 512], F32)
                    kps = PS(r4, "kps_%d" % seq, [128, 512], F32)
                    vps = PS(r4, "vps_%d" % seq, [128, 512], F32)
                    gps = PS(r4, "gps_%d" % seq, [128, 8], F32)
                    tpss = [tps0, tps1]
                    X0, V0, A0, T0 = dX.n, sV.n, sA.n, sT.n
                    KV, KA, KT = 5, 3, 2

                    def Vc(i, s):
                        return V0 + KV * i + s

                    def Ac(i, s):
                        return A0 + KA * i + s

                    def Tc(i, s):
                        return T0 + KT * i + s

                    with nc.Block() as b:
                        @b.sync
                        def _(s):
                            for i in range(NT):
                                if i >= 2:
                                    s.wait_ge(sV.h, Vc(i - 2, 2))
                                ins = s.dma_start(xbs[i % 2][:, :],
                                                  xv[tok0 + i * 128: tok0 + (i + 1) * 128, :])
                                dX.inc(ins, 16)

                        @b.tensor
                        def _(t):
                            for i in range(NT):
                                t.wait_ge(sV.h, Vc(i, 3))
                                if i >= 2:
                                    t.wait_ge(sV.h, Vc(i - 2, 4))
                                for kc in range(KC):
                                    t.transpose(
                                        tpss[i % 2][:, kc * 128:(kc + 1) * 128],
                                        tbufs[i % 2][:, kc * 128:(kc + 1) * 128],
                                        idn[:, :])
                                fence(t, sT)
                                t.wait_ge(sV.h, Vc(i, 4))
                                if i >= 1:
                                    t.wait_ge(sV.h, Vc(i - 1, 5))
                                for pr in range(4):
                                    for kc in range(KC):
                                        t.matmul(qps[:, pr * 128:(pr + 1) * 128],
                                                 wqb[:, kc, pr * 128:(pr + 1) * 128],
                                                 tTs[i % 2][:, kc, :],
                                                 start=(kc == 0), stop=(kc == KC - 1))
                                    for kc in range(KC):
                                        t.matmul(kps[:, pr * 128:(pr + 1) * 128],
                                                 wkb[:, kc, pr * 128:(pr + 1) * 128],
                                                 tTs[i % 2][:, kc, :],
                                                 start=(kc == 0), stop=(kc == KC - 1))
                                for kc in range(KC):
                                    t.matmul(vps[:, :], tTs[i % 2][:, kc, :],
                                             wvb[:, kc, :],
                                             start=(kc == 0), stop=(kc == KC - 1))
                                for kc in range(KC):
                                    t.matmul(gps[:, :], tTs[i % 2][:, kc, :],
                                             wgb[:, kc, :],
                                             start=(kc == 0), stop=(kc == KC - 1))
                                fence(t, sT)

                        @b.vector
                        def _(v):
                            for i in range(NT):
                                xb = xbs[i % 2]
                                hb = hbufs[i % 2]
                                v.wait_ge(dX.h, X0 + 16 * (i + 1))
                                if i == 0:
                                    v.wait_ge(dW.h, bc_done)
                                v.reduce_sum(ssums[i % 2][:, :], xb[:, :], axis=AX.X)
                                v.scalar_tensor_tensor(
                                    sqscr[:, :], xb[:, :], 1.0, xb[:, :],
                                    op0=OP.mult, op1=OP.mult,
                                    accum_out=ssqs[i % 2][:, :])
                                fence(v, sV)
                                v.wait_ge(sA.h, Ac(i, 1))
                                v.tensor_scalar(hb[:, :], xb[:, :], mns[i % 2][:, 0:1],
                                                rsvs[i % 2][:, 0:1],
                                                op0=OP.subtract, op1=OP.mult)
                                v.tensor_tensor(hb[:, :], hb[:, :], CA1[:, :], op=OP.mult)
                                v.tensor_tensor(hb[:, :], hb[:, :], SAr[:, :], op=OP.add)
                                v.scalar_tensor_tensor(
                                    sqscr[:, :], hb[:, :], 1.0, hb[:, :],
                                    op0=OP.mult, op1=OP.mult,
                                    accum_out=ssqs[i % 2][:, :])
                                fence(v, sV)
                                v.wait_ge(sA.h, Ac(i, 2))
                                v.tensor_scalar(tbufs[i % 2][:, :], hb[:, :],
                                                rsqs[i % 2][:, 0:1], None, op0=OP.mult)
                                fence(v, sV)
                                v.wait_ge(sT.h, Tc(i, 1))
                                v.tensor_copy(
                                    tTs[i % 2][:, :, :].rearrange("p c n -> p (c n)"),
                                    tpss[i % 2][:, :])
                                fence(v, sV)
                                v.wait_ge(sT.h, Tc(i, 2))
                                v.wait_ge(sA.h, Ac(i, 3))
                                for pr in range(4):
                                    v.tensor_copy(qTs[:, pr, i * 128:(i + 1) * 128],
                                                  qps[:, pr * 128:(pr + 1) * 128])
                                    v.tensor_copy(kTs[:, pr, i * 128:(i + 1) * 128],
                                                  kps[:, pr * 128:(pr + 1) * 128])
                                v.tensor_scalar(gtmp[:, :], gtmp[:, :], 1.0, None, op0=OP.add)
                                v.tensor_copy(vTs[:, i, :], vps[:, :])
                                v.reciprocal(gSig[:, i, :], gtmp[:, :])
                                fence(v, sV)

                        @b.scalar
                        def _(a):
                            for i in range(NT):
                                a.wait_ge(sV.h, Vc(i, 1))
                                a.activation(mns[i % 2][:, :], ssums[i % 2][:, :],
                                             ACT.Copy, scale=c1d[:, 0:1])
                                a.activation(am2[:, :], ssums[i % 2][:, :],
                                             ACT.Square, scale=c1d[:, 0:1])
                                a.drain()
                                a.activation(anm2[:, :], am2[:, :],
                                             ACT.Copy, scale=negone[:, 0:1],
                                             bias=1e-5)
                                a.drain()
                                a.activation(avrp[:, :], ssqs[i % 2][:, :],
                                             ACT.Ln, scale=c1d[:, 0:1],
                                             bias=anm2[:, 0:1])
                                a.drain()
                                a.activation(rsvs[i % 2][:, :], avrp[:, :],
                                             ACT.Exp, scale=nhalf[:, 0:1])
                                fence(a, sA)
                                a.wait_ge(sV.h, Vc(i, 2))
                                a.activation(rsqs[i % 2][:, :], ssqs[i % 2][:, :],
                                             ACT.Ln, scale=c1d[:, 0:1],
                                             bias=epsrms[:, 0:1])
                                a.drain()
                                a.activation(rsqs[i % 2][:, :], rsqs[i % 2][:, :],
                                             ACT.Exp, scale=nhalf[:, 0:1])
                                fence(a, sA)
                                a.wait_ge(sT.h, Tc(i, 2))
                                a.activation(gtmp[:, :], gps[:, :], ACT.Exp,
                                             scale=negone[:, 0:1])
                                fence(a, sA)

                if debug and seq == 0:
                    d_ca = mk_dbg("CA1", [128, D], F32)
                    d_t0 = mk_dbg("tb0", [128, D], BF16)
                    d_h0 = mk_dbg("h0", [128, D], F32)
                    d_qt = mk_dbg("qT", [128, 4, 256], BF16)
                    d_vt = mk_dbg("vT", [128, 512], BF16)
                    d_gs = mk_dbg("gS", [128, 8], F32)
                    with nc.Block() as b:
                        @b.gpsimd
                        def _(g):
                            g.dma_start(d_ca[:, :], CA1[:, :]).then_inc(dY.h, 16); dY.n += 16
                            g.dma_start(d_t0[:, :], tbufs[0][:, :]).then_inc(dY.h, 16); dY.n += 16
                            g.dma_start(d_h0[:, :], hbufs[0][:, :]).then_inc(dY.h, 16); dY.n += 16
                            g.dma_start(d_qt[:, :, :], qTs[:, :, 0:256]).then_inc(dY.h, 16); dY.n += 16
                            g.dma_start(d_vt[:, :], vTs[:, 0, :]).then_inc(dY.h, 16); dY.n += 16
                            g.dma_start(d_gs[:, :], gSig[:, 0, :]).then_inc(dY.h, 16); dY.n += 16

                with contextlib.ExitStack() as r5:
                    Sps = PS(r5, "Sps_%d" % seq, [128, N], F32)
                    oTps = PS(r5, "oTps_%d" % seq, [64, 128], F32)
                    V0, A0, T0 = sV.n, sA.n, sT.n
                    KV, KA, KT = 4, 1, 3

                    def Vc(k, s):
                        return V0 + KV * k + s

                    def Ac(k, s):
                        return A0 + KA * k + s

                    def Tc(k, s):
                        return T0 + KT * k + s

                    iters = [(i, h) for i in range(NT) for h in range(H)]

                    with nc.Block() as b:
                        @b.tensor
                        def _(t):
                            for k, (i, h) in enumerate(iters):
                                nkb = i + 1
                                pr, par = h // 2, h % 2
                                tp = (64 * par, 0)
                                if k >= 1:
                                    t.wait_ge(sV.h, Vc(k - 1, 3))
                                for kb in range(nkb):
                                    ins = t.matmul(
                                        Sps[:, kb * 128:(kb + 1) * 128],
                                        qTs[64 * par:64 * par + 64, pr, i * 128:(i + 1) * 128],
                                        kTs[64 * par:64 * par + 64, pr, kb * 128:(kb + 1) * 128],
                                        start=True, stop=True, tile_position=tp)
                                sT.inc(ins)
                                t.wait_ge(sV.h, Vc(k, 2))
                                for kb in range(nkb):
                                    ins = t.matmul(
                                        Sps[:, kb * 128:(kb + 1) * 128],
                                        Psb[:, kb * 128:(kb + 1) * 128],
                                        diag[:, :], start=True, stop=True)
                                sT.inc(ins)
                                t.wait_ge(sV.h, Vc(k, 3))
                                if k >= 1:
                                    t.wait_ge(sV.h, Vc(k - 1, 4))
                                for kb in range(nkb):
                                    ins = t.matmul(
                                        oTps[:, :],
                                        vTs[:, kb, h * 64:(h + 1) * 64],
                                        ptpsb[:, kb * 128:(kb + 1) * 128],
                                        start=(kb == 0), stop=(kb == nkb - 1))
                                sT.inc(ins)

                        @b.vector
                        def _(v):
                            for k, (i, h) in enumerate(iters):
                                nkb = i + 1
                                v.wait_ge(sT.h, Tc(k, 1))
                                v.tensor_tensor(Sps[:, i * 128:(i + 1) * 128],
                                                Sps[:, i * 128:(i + 1) * 128],
                                                msk[:, :], op=OP.add)
                                v.reduce_max(nbias[:, :], Sps[:, :nkb * 128],
                                             axis=AX.X, negate=True)
                                fence(v, sV)
                                v.wait_ge(sA.h, Ac(k, 1))
                                if k >= 1:
                                    v.wait_ge(sT.h, Tc(k - 1, 2))
                                v.reciprocal(rrs[:, :], rsum[:, :])
                                v.tensor_tensor(diag[:, :], idn[:, :],
                                                gSig[:, i, h:h + 1]
                                                .broadcast_to([128, 128]), op=OP.mult)
                                v.tensor_tensor(diag[:, :], diag[:, :],
                                                rrs[:, 0:1]
                                                .broadcast_to([128, 128]),
                                                op=OP.mult)
                                fence(v, sV)
                                v.wait_ge(sT.h, Tc(k, 2))
                                ins = v.tensor_copy(ptpsb[:, :nkb * 128],
                                                    Sps[:, :nkb * 128])
                                sV.inc(ins)
                                v.wait_ge(sT.h, Tc(k, 3))
                                ins = v.tensor_copy(
                                    outTs[64 * (h % 2):64 * (h % 2) + 64, h // 2,
                                          i * 128:(i + 1) * 128],
                                    oTps[:, :])
                                sV.inc(ins)

                        @b.scalar
                        def _(a):
                            for k, (i, h) in enumerate(iters):
                                nkb = i + 1
                                a.wait_ge(sV.h, Vc(k, 1))
                                if k >= 1:
                                    a.wait_ge(sT.h, Tc(k - 1, 2))
                                    a.wait_ge(sV.h, Vc(k - 1, 2))
                                a.activation(Psb[:, :nkb * 128],
                                             Sps[:, :nkb * 128],
                                             ACT.Exp, bias=nbias[:, 0:1],
                                             accum_out=rsum[:, :])
                                fence(a, sA)

                with contextlib.ExitStack() as r6:
                    pps0 = PS(r6, "pps0_%d" % seq, [128, 512], F32)
                    pps1 = PS(r6, "pps1_%d" % seq, [128, 512], F32)
                    ppss = [pps0, pps1]
                    with nc.Block() as b:
                        @b.gpsimd
                        def _(g):
                            g.wait_ge(dY.h, mod_stored)
                            g.dma_start(GAr[:, :], modv[2 * 2 + seq:2 * 2 + seq + 1, :]
                                        .broadcast_to([128, D])).then_inc(dW.h, 16); dW.n += 16
                    ga_done = dW.n

                    X0, V0, A0, T0, G0 = dX.n, sV.n, sA.n, sT.n, dY.n
                    KV, KA, KT = 1, 1, 1

                    def Vc(i, s):
                        return V0 + i + s

                    def Ac(i, s):
                        return A0 + i + s

                    def Tc(i, s):
                        return T0 + i + s

                    with nc.Block() as b:
                        @b.sync
                        def _(s):
                            for i in range(NT):
                                if i >= 2:
                                    s.wait_ge(sV.h, Vc(i - 2, 1))
                                ins = s.dma_start(xbs[i % 2][:, :],
                                                  xv[tok0 + i * 128: tok0 + (i + 1) * 128, :])
                                dX.inc(ins, 16)

                        @b.tensor
                        def _(t):
                            for i in range(NT):
                                if i >= 1:
                                    t.wait_ge(sV.h, Vc(i - 1, 1))
                                for dh in range(2):
                                    for kc in range(4):
                                        ins = t.matmul(
                                            ppss[dh][:, :],
                                            outTs[:, kc, i * 128:(i + 1) * 128],
                                            wob[:, kc, dh * 512:(dh + 1) * 512],
                                            start=(kc == 0), stop=(kc == 3))
                                sT.inc(ins)

                        @b.vector
                        def _(v):
                            for i in range(NT):
                                gt = seq * NT + i
                                v.wait_ge(sT.h, Tc(i, 1))
                                v.wait_ge(dX.h, X0 + 16 * (i + 1))
                                if i == 0:
                                    v.wait_ge(dW.h, ga_done)
                                if i >= 2:
                                    v.wait_ge(dY.h, G0 + 16 * (i - 1))
                                    v.wait_ge(sA.h, Ac(i - 2, 1))
                                x1b = x1bs[i % 2]
                                for dh in range(2):
                                    sl = slice(dh * 512, (dh + 1) * 512)
                                    v.tensor_tensor(x1b[:, sl], GAr[:, sl],
                                                    ppss[dh][:, :], op=OP.mult)
                                v.tensor_tensor(x1b[:, :], x1b[:, :],
                                                xbs[i % 2][:, :], op=OP.add)
                                v.reduce_sum(ssums[i % 2][:, :], x1b[:, :], axis=AX.X)
                                v.scalar_tensor_tensor(
                                    sqscr[:, :], x1b[:, :], 1.0, x1b[:, :],
                                    op0=OP.mult, op1=OP.mult,
                                    accum_out=ssqs[i % 2][:, :])
                                fence(v, sV)

                        @b.scalar
                        def _(a):
                            for i in range(NT):
                                gt = seq * NT + i
                                a.wait_ge(sV.h, Vc(i, 1))
                                a.activation(mn_tiles[gt][:, :], ssums[i % 2][:, :],
                                             ACT.Copy, scale=c1d[:, 0:1])
                                a.activation(am2[:, :], ssums[i % 2][:, :],
                                             ACT.Square, scale=c1d[:, 0:1])
                                a.drain()
                                a.activation(anm2[:, :], am2[:, :],
                                             ACT.Copy, scale=negone[:, 0:1],
                                             bias=1e-5)
                                a.drain()
                                a.activation(avrp[:, :], ssqs[i % 2][:, :],
                                             ACT.Ln, scale=c1d[:, 0:1],
                                             bias=anm2[:, 0:1])
                                a.drain()
                                a.activation(rsv_tiles[gt][:, :], avrp[:, :],
                                             ACT.Exp, scale=nhalf[:, 0:1])
                                fence(a, sA)

                        @b.gpsimd
                        def _(g):
                            for i in range(NT):
                                g.wait_ge(sA.h, Ac(i, 1))
                                ins = g.dma_start(
                                    x1d[tok0 + i * 128: tok0 + (i + 1) * 128, :],
                                    x1bs[i % 2][:, :])
                                dY.inc(ins, 16)

        x1_stored = dY.n
        attn_es.close()

        NGS = [(0, 512), (512, 512), (1024, 512), (1536, 512), (2048, 512), (2560, 256)]
        with contextlib.ExitStack() as r7:
            wgq_sb = SB(r7, "wgq_sb", [128, KC, FF], BF16)
            wuq_sb = SB(r7, "wuq_sb", [128, KC, FF], BF16)
            wdq_sb = SB(r7, "wdq_sb", [128, FKC, D], BF16)
            x1t0 = SB(r7, "x1t0", [128, D], F32)
            x1t1 = SB(r7, "x1t1", [128, D], F32)
            h2b = SB(r7, "h2b", [128, D], F32)
            h2q = SB(r7, "h2q", [128, D], BF16)
            h2qT = SB(r7, "h2qT", [128, KC, 128], BF16)
            sg0 = SB(r7, "sg0", [128, 512], F32)
            sg1 = SB(r7, "sg1", [128, 512], F32)
            hfb = SB(r7, "hfb", [128, 512], F32)
            hfq = SB(r7, "hfq", [128, FF], BF16)
            hfqT = SB(r7, "hfqT", [128, FKC, 128], BF16)
            outb = SB(r7, "outb", [128, D], F32)
            q_xn = SB(r7, "q_xn", [128, D], F32)
            q_aa = SB(r7, "q_aa", [128, D], F32)
            q_sg = SB(r7, "q_sg", [128, D], F32)
            q_vv = SB(r7, "q_vv", [128, D], F32)
            q_bm = SB(r7, "q_bm", [128, 64], F32)
            q_sc = SB(r7, "q_sc", [128, 64], F32)
            q_rs = SB(r7, "q_rs", [128, 64], F32)
            q_tr = SB(r7, "q_tr", [128, 896], F32)
            q_rex = SB(r7, "q_rex", [128, D], F32)
            t2ps = PS(r7, "t2ps", [128, 1024], BF16)
            gatep = PS(r7, "gatep", [128, 512], F32)
            upp = PS(r7, "upp", [128, 512], F32)
            hftp0 = PS(r7, "hftp0", [128, 1024], BF16)
            hftp1 = PS(r7, "hftp1", [128, 1024], BF16)
            dnp0 = PS(r7, "dnp0", [128, 512], F32)
            dnp1 = PS(r7, "dnp1", [128, 512], F32)
            q2 = {"xn": q_xn, "aa": q_aa, "sg": q_sg, "vv": q_vv,
                  "bmax": q_bm, "scale": q_sc, "rs": q_rs,
                  "tr": q_tr, "rex": q_rex}
            x1ts = [x1t0, x1t1]
            sgs = [sg0, sg1]
            hftps = [hftp0, hftp1]
            dnps = [dnp0, dnp1]

            with nc.Block() as b:
                @b.sync
                def _(s):
                    s.wait_ge(dY.h, ffnw_stored)
                    for kc in range(KC):
                        s.dma_start(wgq_sb[:, kc, :], wgq_d[kc, :, :]).then_inc(dX.h, 16); dX.n += 16
                        s.dma_start(wuq_sb[:, kc, :], wuq_d[kc, :, :]).then_inc(dX.h, 16); dX.n += 16
                    for kc in range(FKC):
                        s.dma_start(wdq_sb[:, kc, :], wdq_d[kc, :, :]).then_inc(dX.h, 16); dX.n += 16

                @b.gpsimd
                def _(g):
                    g.dma_start(SBr[:, :], modv[3 * 2 + 0:3 * 2 + 1, :]
                                .broadcast_to([128, D])).then_inc(dW.h, 16); dW.n += 16
                    g.dma_start(CB1[:, :], modv[4 * 2 + 0:4 * 2 + 1, :]
                                .broadcast_to([128, D])).then_inc(dW.h, 16); dW.n += 16
                    g.dma_start(GBr[:, :], modv[5 * 2 + 0:5 * 2 + 1, :]
                                .broadcast_to([128, D])).then_inc(dW.h, 16); dW.n += 16
            w_done = dX.n
            bc0_done = dW.n

            X0, V0, A0, T0, G0 = dX.n, sV.n, sA.n, sT.n, dY.n
            KV, KA, KT, NG = 12, 6, 11, 6

            def Vc(i, s):
                return V0 + KV * i + s

            def Ac(i, s):
                return A0 + KA * i + s

            def Tc(i, s):
                return T0 + KT * i + s

            TGRP = [(0, 8), (8, 16), (16, 22)]

            with nc.Block() as b:
                @b.scalar
                def _(a):
                    a.add_instruction(mybir.InstLoadActFuncSet(
                        name=nc.get_next_instruction_name(),
                        act_func_set_id=18, ins=[], outs=[]))

            with nc.Block() as b:
                @b.sync
                def _(s):
                    s.wait_ge(dY.h, x1_stored)
                    for i in range(32):
                        if i >= 2:
                            s.wait_ge(sV.h, Vc(i - 2, 12))
                        ins = s.dma_start(x1ts[i % 2][:, :],
                                          x1d[i * 128:(i + 1) * 128, :])
                        dX.inc(ins, 16)

                @b.gpsimd
                def _(g):
                    for i in range(16):
                        g.wait_ge(sV.h, Vc(i, 12))
                        ins = g.dma_start(yv[i * 128:(i + 1) * 128, :], outb[:, :])
                        dY.inc(ins, 16)
                    g.dma_start(SBr[:, :], modv[3 * 2 + 1:3 * 2 + 2, :]
                                .broadcast_to([128, D])).then_inc(dW.h, 16); dW.n += 16
                    g.dma_start(CB1[:, :], modv[4 * 2 + 1:4 * 2 + 2, :]
                                .broadcast_to([128, D])).then_inc(dW.h, 16); dW.n += 16
                    g.dma_start(GBr[:, :], modv[5 * 2 + 1:5 * 2 + 2, :]
                                .broadcast_to([128, D])).then_inc(dW.h, 16); dW.n += 16
                    for i in range(16, 32):
                        g.wait_ge(sV.h, Vc(i, 12))
                        ins = g.dma_start(yv[i * 128:(i + 1) * 128, :], outb[:, :])
                        dY.inc(ins, 16)
                    g.wait_ge(dY.h, dY.n)

                bc1_done = dW.n

                @b.vector
                def _(v):
                    for i in range(32):
                        v.wait_ge(dX.h, X0 + 16 * (i + 1))
                        if i == 0:
                            v.wait_ge(dW.h, bc0_done)
                        if i == 16:
                            v.wait_ge(dW.h, bc1_done)
                        if i >= 1:
                            v.wait_ge(sT.h, Tc(i - 1, 7))
                        v.tensor_scalar(h2b[:, :], x1ts[i % 2][:, :],
                                        mn_tiles[i][:, 0:1], rsv_tiles[i][:, 0:1],
                                        op0=OP.subtract, op1=OP.mult)
                        v.tensor_tensor(h2b[:, :], h2b[:, :], CB1[:, :], op=OP.mult)
                        v.tensor_tensor(h2b[:, :], h2b[:, :], SBr[:, :], op=OP.add)
                        emit_quant(v, h2b[:, :], h2q[:, :], q2, D, D // 16)
                        ins = v.engine_nop()
                        sV.inc(ins)
                        v.wait_ge(sT.h, Tc(i, 1))
                        ins = v.tensor_copy(
                            h2qT[:, :, :].rearrange("p c n -> p (c n)")[:, :1024],
                            t2ps[:, :])
                        sV.inc(ins)
                        for ngi, (c0, w) in enumerate(NGS):
                            v.wait_ge(sA.h, Ac(i, 1 + ngi))
                            v.wait_ge(sT.h, Tc(i, 2 + ngi))
                            if i >= 1:
                                v.wait_ge(sT.h, Tc(i - 1, 8 + ngi // 2))
                            v.tensor_tensor(hfb[:, :w], sgs[ngi % 2][:, :w],
                                            upp[:, :w], op=OP.mult)
                            emit_quant(v, hfb[:, :w], hfq[:, c0:c0 + w], q2, w, w // 16)
                            ins = v.engine_nop()
                            sV.inc(ins)
                        for gi, (g0, g1) in enumerate(TGRP):
                            v.wait_ge(sT.h, Tc(i, 8 + gi))
                            ins = v.tensor_copy(
                                hfqT[:, :, :].rearrange("p c n -> p (c n)")
                                [:, g0 * 128:g1 * 128],
                                hftps[gi % 2][:, :(g1 - g0) * 128])
                            sV.inc(ins)
                        v.wait_ge(sT.h, Tc(i, 11))
                        if i >= 1:
                            v.wait_ge(dY.h, G0 + 16 * i)
                        for dh in range(2):
                            sl = slice(dh * 512, (dh + 1) * 512)
                            v.tensor_tensor(outb[:, sl], GBr[:, sl],
                                            dnps[dh][:, :], op=OP.mult)
                        ins = v.tensor_tensor(outb[:, :], outb[:, :],
                                              x1ts[i % 2][:, :], op=OP.add)
                        sV.inc(ins)

                @b.tensor
                def _(t):
                    for i in range(32):
                        t.wait_ge(sV.h, Vc(i, 1))
                        if i >= 1:
                            t.wait_ge(sV.h, Vc(i - 1, 2))
                        for kc in range(KC):
                            ins = t.transpose(t2ps[:, kc * 128:(kc + 1) * 128],
                                              h2q[:, kc * 128:(kc + 1) * 128],
                                              idn[:, :])
                        sT.inc(ins)
                        if i == 0:
                            t.wait_ge(dX.h, w_done)
                        t.wait_ge(sV.h, Vc(i, 2))
                        for ngi, (c0, w) in enumerate(NGS):
                            if i >= 1 or ngi >= 1:
                                pi, png = (i, ngi - 1) if ngi >= 1 else (i - 1, 5)
                                t.wait_ge(sA.h, Ac(pi, 1 + png))
                                t.wait_ge(sV.h, Vc(pi, 3 + png))
                            for kc in range(KC):
                                t.matmul(gatep[:, :w], h2qT[:, kc, :],
                                         wgq_sb[:, kc, c0:c0 + w],
                                         start=(kc == 0), stop=(kc == KC - 1))
                            for kc in range(KC):
                                ins = t.matmul(upp[:, :w], h2qT[:, kc, :],
                                               wuq_sb[:, kc, c0:c0 + w],
                                               start=(kc == 0), stop=(kc == KC - 1))
                            sT.inc(ins)
                        for gi, (g0, g1) in enumerate(TGRP):
                            t.wait_ge(sV.h, Vc(i, 3 + (2 * gi + 1)))
                            if gi == 2:
                                t.wait_ge(sV.h, Vc(i, 9))
                            elif i >= 1:
                                t.wait_ge(sV.h, Vc(i - 1, 11 if gi == 0 else 10))
                            for kc in range(g0, g1):
                                ins = t.transpose(
                                    hftps[gi % 2][:, (kc - g0) * 128:(kc - g0 + 1) * 128],
                                    hfq[:, kc * 128:(kc + 1) * 128],
                                    idn[:, :])
                            sT.inc(ins)
                        t.wait_ge(sV.h, Vc(i, 11))
                        if i >= 1:
                            t.wait_ge(sV.h, Vc(i - 1, 12))
                        for dh in range(2):
                            for kc in range(FKC):
                                ins = t.matmul(dnps[dh][:, :], hfqT[:, kc, :],
                                               wdq_sb[:, kc, dh * 512:(dh + 1) * 512],
                                               start=(kc == 0), stop=(kc == FKC - 1))
                        sT.inc(ins)

                @b.scalar
                def _(a):
                    for i in range(32):
                        for ngi, (c0, w) in enumerate(NGS):
                            a.wait_ge(sT.h, Tc(i, 2 + ngi))
                            if i >= 1 or ngi >= 2:
                                pi, png = (i, ngi - 2) if ngi >= 2 else (i - 1, 4 + ngi)
                                a.wait_ge(sV.h, Vc(pi, 3 + png))
                            ins = a.activation(sgs[ngi % 2][:, :w], gatep[:, :w],
                                               ACT.Silu)
                            sA.inc(ins)

    return nc


_NC_CACHE = None


def kernel(L0_attn_tokens, L0_attn_wq, L0_attn_wk, L0_attn_wv, L0_attn_wo,
           L0_attn_w_lr, L0_attn_w_target, L0_attn_w_gates, L0_attn_rms_w,
           L0_ffn_w_gate, L0_ffn_w_up, L0_ffn_w_down, L0_c, L0_wmod):
    global _NC_CACHE
    if _NC_CACHE is None:
        _NC_CACHE = build_program()
    nc = _NC_CACHE

    x = np.ascontiguousarray(L0_attn_tokens, dtype=np.float32)
    iden = np.eye(128, dtype=np.float32).astype(BF)
    cmask = np.where(np.triu(np.ones((128, 128), dtype=bool), 1),
                     np.float32(NEG_BIG), np.float32(0.0))
    shared = {
        "wq": np.ascontiguousarray(L0_attn_wq, dtype=np.float32),
        "wk": np.ascontiguousarray(L0_attn_wk, dtype=np.float32),
        "wv": np.ascontiguousarray(L0_attn_wv, dtype=np.float32),
        "wo": np.ascontiguousarray(L0_attn_wo, dtype=np.float32),
        "wgates": np.ascontiguousarray(L0_attn_w_gates, dtype=np.float32),
        "rmsw": np.ascontiguousarray(L0_attn_rms_w, dtype=np.float32),
        "wmod": np.ascontiguousarray(L0_wmod, dtype=np.float32),
        "wgate": np.ascontiguousarray(L0_ffn_w_gate, dtype=np.float32),
        "wup": np.ascontiguousarray(L0_ffn_w_up, dtype=np.float32),
        "wdown": np.ascontiguousarray(L0_ffn_w_down, dtype=np.float32),
        "iden": iden,
        "cmask": cmask,
        "cvals": np.stack([
            np.full(128, 1e-5, np.float32),
            np.full(128, -0.5, np.float32),
            np.full(128, -1.0, np.float32),
            np.full(128, 0.125, np.float32),
            np.full(128, 1.0 / D, np.float32),
            np.full(128, EPS_RMS, np.float32),
            np.full(128, -0.25, np.float32),
            np.zeros(128, np.float32),
        ], axis=1),
    }
    cfull = np.ascontiguousarray(L0_c, dtype=np.float32)
    in_maps = []
    for core in range(8):
        m = dict(shared)
        m["x"] = np.ascontiguousarray(x[2 * core:2 * core + 2])
        m["cvec"] = np.ascontiguousarray(cfull[2 * core:2 * core + 2])
        in_maps.append(m)

    res = run_bass_kernel_spmd(nc, in_maps, core_ids=list(range(8)))
    out = np.concatenate([np.asarray(r["y"], dtype=np.float32)
                          for r in res.results], axis=0)
    return out


# version 06 (final)
# speedup vs baseline: 1.1241x; 1.1241x over previous
import contextlib
import numpy as np
import ml_dtypes

import concourse.bass as bass
import concourse.mybir as mybir
from concourse.bass_utils import run_bass_kernel_spmd

F32 = mybir.dt.float32
BF16 = mybir.dt.bfloat16
AX = mybir.AxisListType
OP = mybir.AluOpType
ACT = mybir.ActivationFunctionType
BF = ml_dtypes.bfloat16

D = 1024
N = 2048
H = 8
E = 64
FF = 2816
NT = 16
KC = 8
FKC = 22
EPS_RMS = float(np.finfo(np.float32).eps)
NEG_BIG = -float(np.finfo(np.float32).max)

THR_U = [0.25, 0.75, 1.25, 1.75]
THR_V = [2.5, 3.5, 5.0, 5.0]


class Sem:
    def __init__(self, stack, nc, name):
        self.h = stack.enter_context(nc.semaphore(name))
        self.n = 0

    def inc(self, instr, k=1):
        instr.then_inc(self.h, k)
        self.n += k
        return self.n


def fence(eng, sem, delay=384):
    if hasattr(eng, "engine_nop"):
        return sem.inc(eng.engine_nop())
    return sem.inc(eng.drain())


def emit_quant(v, X, out, q, n, nb):
    Xb = X.rearrange("p (nb b) -> p nb b", b=16)
    xn = q["xn"][:, :n]
    aa = q["aa"][:, :n]
    sg = q["sg"][:, :n]
    vv = q["vv"][:, :n]
    v.tensor_reduce(q["bmax"][:, :nb], Xb, axis=AX.X, op=OP.max,
                    apply_absolute_value=True)
    v.scalar_tensor_tensor(aa, X, -1.0, X, op0=OP.mult, op1=OP.max)
    v.tensor_scalar(q["scale"][:, :nb], q["bmax"][:, :nb], 1.0 / 6.0, 1e-12,
                    op0=OP.mult, op1=OP.max)
    v.tensor_scalar(sg, X, 0.0, 2.0, op0=OP.is_ge, op1=OP.mult)
    v.reciprocal(q["rs"][:, :nb], q["scale"][:, :nb])
    v.tensor_scalar(sg, sg, 1.0, None, op0=OP.subtract)
    rsb = q["rs"][:, :nb].unsqueeze(2).broadcast_to([128, nb, 16])
    scb = q["scale"][:, :nb].unsqueeze(2).broadcast_to([128, nb, 16])
    aab = aa.rearrange("p (nb b) -> p nb b", b=16)
    xnb2 = xn.rearrange("p (nb b) -> p nb b", b=16)
    v.tensor_tensor(xnb2, aab, rsb, op=OP.mult)
    v.tensor_scalar(aa, xn, THR_U[0], None, op0=OP.is_gt)
    for t in THR_U[1:]:
        v.scalar_tensor_tensor(aa, xn, t, aa, op0=OP.is_gt, op1=OP.add)
    v.tensor_scalar(vv, xn, THR_V[0], None, op0=OP.is_gt)
    for t in THR_V[1:]:
        v.scalar_tensor_tensor(vv, xn, t, vv, op0=OP.is_gt, op1=OP.add)
    v.scalar_tensor_tensor(aa, aa, 0.5, vv, op0=OP.mult, op1=OP.add)
    v.tensor_tensor(aa, aa, sg, op=OP.mult)
    ob = out.rearrange("p (nb b) -> p nb b", b=16)
    ab = aa.rearrange("p (nb b) -> p nb b", b=16)
    v.tensor_tensor(ob, ab, scb, op=OP.mult)


def build_program(debug=False):
    nc = bass.Bass("TRN2", target_bir_lowering=False,
                   detect_race_conditions=False)
    dbg = {}

    def mk_dbg(name, shape, dt):
        dbg[name] = nc.dram_tensor("dbg_" + name, shape, dt, kind="ExternalOutput")
        return dbg[name]

    x_in = nc.dram_tensor("x", [2, N, D], F32, kind="ExternalInput")
    wq_in = nc.dram_tensor("wq", [H, D, E], F32, kind="ExternalInput")
    wk_in = nc.dram_tensor("wk", [H, D, E], F32, kind="ExternalInput")
    wv_in = nc.dram_tensor("wv", [H, D, E], F32, kind="ExternalInput")
    wo_in = nc.dram_tensor("wo", [H, E, D], F32, kind="ExternalInput")
    wg_in = nc.dram_tensor("wgates", [D, H], F32, kind="ExternalInput")
    rms_in = nc.dram_tensor("rmsw", [D], F32, kind="ExternalInput")
    c_in = nc.dram_tensor("cvec", [2, D], F32, kind="ExternalInput")
    wmod_in = nc.dram_tensor("wmod", [D, 6 * D], F32, kind="ExternalInput")
    wgate_in = nc.dram_tensor("wgate", [FF, D], F32, kind="ExternalInput")
    wup_in = nc.dram_tensor("wup", [FF, D], F32, kind="ExternalInput")
    wdown_in = nc.dram_tensor("wdown", [D, FF], F32, kind="ExternalInput")
    iden_in = nc.dram_tensor("iden", [128, 128], BF16, kind="ExternalInput")
    mask_in = nc.dram_tensor("cmask", [128, 128], F32, kind="ExternalInput")
    cvals_in = nc.dram_tensor("cvals", [128, 8], F32, kind="ExternalInput")
    y_out = nc.dram_tensor("y", [2, N, D], F32, kind="ExternalOutput")

    xv = x_in.rearrange("s n d -> (s n) d")
    yv = y_out.rearrange("s n d -> (s n) d")

    modv = nc.dram_tensor("modv", [12, D], F32)
    x1d = nc.dram_tensor("x1d", [2 * N, D], F32)
    wgq_d = nc.dram_tensor("wgq", [KC, 128, FF], BF16)
    wuq_d = nc.dram_tensor("wuq", [KC, 128, FF], BF16)
    wdq_d = nc.dram_tensor("wdq", [FKC, 128, D], BF16)

    st = contextlib.ExitStack()
    with st:
        def SB(stack, name, shape, dt):
            return stack.enter_context(nc.sbuf_tensor(name, shape, dt))

        def PS(stack, name, shape, dt):
            return stack.enter_context(nc.psum_tensor(name, shape, dt))

        idn = st.enter_context(nc.sbuf_tensor("idn", [128, 128], BF16))
        mn_tiles = [st.enter_context(nc.sbuf_tensor("mnt%d" % g, [128, 1], F32))
                    for g in range(32)]
        rsv_tiles = [st.enter_context(nc.sbuf_tensor("rst%d" % g, [128, 1], F32))
                     for g in range(32)]
        CB1 = st.enter_context(nc.sbuf_tensor("CB1", [128, D], F32))
        SBr = st.enter_context(nc.sbuf_tensor("SBr", [128, D], F32))
        GBr = st.enter_context(nc.sbuf_tensor("GBr", [128, D], F32))
        epsln = st.enter_context(nc.sbuf_tensor("epsln", [128, 1], F32))
        nhalf = st.enter_context(nc.sbuf_tensor("nhalf", [128, 1], F32))
        negone = st.enter_context(nc.sbuf_tensor("negone", [128, 1], F32))
        c0125 = st.enter_context(nc.sbuf_tensor("c0125", [128, 1], F32))
        c1d = st.enter_context(nc.sbuf_tensor("c1d", [128, 1], F32))
        epsrms = st.enter_context(nc.sbuf_tensor("epsrms", [128, 1], F32))
        c025n = st.enter_context(nc.sbuf_tensor("c025n", [128, 1], F32))
        am2 = st.enter_context(nc.sbuf_tensor("am2", [128, 1], F32))
        anm2 = st.enter_context(nc.sbuf_tensor("anm2", [128, 1], F32))
        avrp = st.enter_context(nc.sbuf_tensor("avrp", [128, 1], F32))
        attn_es = contextlib.ExitStack()
        msk = SB(attn_es, "msk", [128, 128], F32)
        wqb = SB(attn_es, "wqb", [128, KC, 512], BF16)
        wkb = SB(attn_es, "wkb", [128, KC, 512], BF16)
        wvb = SB(attn_es, "wvb", [128, KC, 512], BF16)
        wgb = SB(attn_es, "wgb", [128, KC, 8], BF16)
        wob = SB(attn_es, "wob", [128, 4, D], BF16)
        rcols = SB(attn_es, "rcols", [128, KC], F32)
        rca = [SB(attn_es, "rca%d" % kc, [128, 1], F32) for kc in range(KC)]
        CA1 = SB(attn_es, "CA1", [128, D], F32)
        SAr = SB(attn_es, "SAr", [128, D], F32)
        GAr = SB(attn_es, "GAr", [128, D], F32)

        dX = Sem(st, nc, "dX")
        dW = Sem(st, nc, "dW")
        dY = Sem(st, nc, "dY")
        sT = Sem(st, nc, "sT")
        sV = Sem(st, nc, "sV")
        sA = Sem(st, nc, "sA")
        sG = Sem(st, nc, "sG")

        wqv = wq_in.rearrange("h (c p) e -> p c h e", p=128)
        wkv = wk_in.rearrange("h (c p) e -> p c h e", p=128)
        wvv = wv_in.rearrange("h (c p) e -> p c h e", p=128)
        wgv = wg_in.rearrange("(c p) h -> p c h", p=128)
        wov = wo_in.rearrange("h e d -> (h e) d").rearrange("(c p) d -> p c d", p=128)
        rmsv = rms_in.rearrange("(c p) -> p c", p=128)
        cv2 = c_in.rearrange("b (cc p) -> p b cc", p=128)

        with contextlib.ExitStack() as r0:
            csb2 = SB(r0, "csb2", [128, 2, 8], F32)
            e1b = SB(r0, "e1b", [128, 16], F32)
            rrb = SB(r0, "rrb", [128, 16], F32)
            scb2 = SB(r0, "scb2", [128, 2, 8], BF16)
            modT = SB(r0, "modT", [128, 96], F32)
            gsc_a = SB(r0, "gsc_a", [128, 16], F32)
            gsc_s = SB(r0, "gsc_s", [128, 16], F32)
            wmod_sb = SB(r0, "wmod_sb", [128, KC, 6 * D], BF16)
            wraw = SB(r0, "wraw", [128, KC, 512], F32)
            with nc.Block() as b:
                @b.sync
                def _(s):
                    s.dma_start(idn[:, :], iden_in[:, :]).then_inc(dX.h, 16); dX.n += 16
                    s.dma_start(msk[:, :], mask_in[:, :]).then_inc(dX.h, 16); dX.n += 16
                    nc.allow_non_contiguous_dma("tiny strided constant loads")
                    s.dma_start(csb2[:, :, :], cv2, allow_slow_non_contiguous=True
                                ).then_inc(dX.h, 16); dX.n += 16
                    s.dma_start(rcols[:, :], rmsv, allow_slow_non_contiguous=True
                                ).then_inc(dX.h, 16); dX.n += 16
                    for kc in range(KC):
                        s.dma_start(rca[kc][:, :], rmsv[:, kc:kc + 1],
                                    allow_slow_non_contiguous=True
                                    ).then_inc(dX.h, 16); dX.n += 16

                @b.scalar
                def _(a):
                    a.add_instruction(mybir.InstLoadActFuncSet(
                        name=nc.get_next_instruction_name(),
                        act_func_set_id=6, ins=[], outs=[]))

                @b.gpsimd
                def _(g):
                    for kc in range(KC):
                        g.dma_start(wmod_sb[:, kc, :],
                                    wmod_in.rearrange("(c p) n -> p c n", p=128)[:, kc, :]
                                    ).then_inc(dW.h, 16); dW.n += 16
                    for c in range(4):
                        g.dma_start(wob[:, c, :], wov[:, c, :]).then_inc(dW.h, 16); dW.n += 16
                    g.wait_ge(dW.h, dW.n)

            with nc.Block() as b:
                @b.sync
                def _(s):
                    nc.allow_non_contiguous_dma("tiny const loads")
                    for k, tens in enumerate((epsln, nhalf, negone, c0125,
                                              c1d, epsrms, c025n)):
                        s.dma_start(tens[:, :], cvals_in[:, k:k + 1],
                                    allow_slow_non_contiguous=True
                                    ).then_inc(dX.h, 16); dX.n += 16
            base_loads = dX.n

            csf = csb2.rearrange("p b cc -> p (b cc)")
            scf2 = scb2.rearrange("p b cc -> p (b cc)")
            with nc.Block() as b:
                @b.scalar
                def _(a):
                    a.wait_ge(dX.h, base_loads)
                    a.activation(e1b[:, :], csf, ACT.Exp, scale=negone[:, 0:1])
                    a.drain()
                    a.activation(e1b[:, :], e1b[:, :], ACT.Copy, bias=1.0)
                    fence(a, sA)

            with nc.Block() as b:
                @b.vector
                def _(v):
                    v.reciprocal(rrb[:, :], e1b[:, :])
                    fence(v, sV)

            with nc.Block() as b:
                @b.gpsimd
                def _(g):
                    g.tensor_tensor(scf2, csf, rrb[:, :], op=OP.mult)
                    fence(g, sG)

            with contextlib.ExitStack() as r1:
                modp = PS(r1, "modp", [128, 96], F32)
                with nc.Block() as b:
                    @b.tensor
                    def _(t):
                        for n48 in range(48):
                            for kc in range(KC):
                                t.matmul(modp[:, n48 * 2:n48 * 2 + 2],
                                         wmod_sb[:, kc, n48 * 128:(n48 + 1) * 128],
                                         scb2[:, :, kc],
                                         start=(kc == 0), stop=(kc == KC - 1))
                with nc.Block() as b:
                    @b.vector
                    def _(v):
                        v.tensor_copy(modT[:, :], modp[:, :])
                        fence(v, sV)

            with nc.Block() as b:
                @b.vector
                def _(v):
                    v.tensor_scalar(modT[:, 16:32], modT[:, 16:32], 1.0, None,
                                    op0=OP.add)
                    v.tensor_scalar(modT[:, 64:80], modT[:, 64:80], 1.0, None,
                                    op0=OP.add)
                    fence(v, sV)
            for q in (2, 5):
                sl = modT[:, q * 16:(q + 1) * 16]
                with nc.Block() as b:
                    @b.scalar
                    def _(a):
                        a.activation(gsc_a[:, :], sl, ACT.Abs)
                        a.drain()
                        a.activation(gsc_a[:, :], gsc_a[:, :], ACT.Relu,
                                     bias=c025n[:, 0:1])
                        a.drain()
                        a.activation(gsc_a[:, :], gsc_a[:, :], ACT.Copy, bias=0.25)
                        a.activation(gsc_s[:, :], sl, ACT.Sign)
                        fence(a, sA)
                with nc.Block() as b:
                    @b.gpsimd
                    def _(g):
                        g.tensor_tensor(sl, gsc_a[:, :], gsc_s[:, :],
                                        op=OP.mult)
                        fence(g, sG)

            modTv = modT.rearrange("p (x cc b) -> p x cc b", cc=8, b=2)
            modvv = modv.rearrange("r (cc p) -> r p cc", p=128)
            with nc.Block() as b:
                @b.gpsimd
                def _(g):
                    nc.allow_non_contiguous_dma("tiny mod row stores")
                    for q in range(6):
                        for bb in range(2):
                            g.dma_start(modvv[q * 2 + bb, :, :],
                                        modTv[:, q, :, bb],
                                        allow_slow_non_contiguous=True
                                        ).then_inc(dY.h, 16); dY.n += 16
            mod_stored = dY.n

            for (win, wdst, ncols) in ((wqv, wqb, 512), (wkv, wkb, 512),
                                       (wvv, wvb, 512), (wgv, wgb, 8)):
                with nc.Block() as b:
                    @b.sync
                    def _(s):
                        for kc in range(KC):
                            if ncols == 512:
                                dst = wraw[:, kc, :ncols].rearrange(
                                    "p (h e) -> p h e", e=E)
                                srcv = win[:, kc, :, :]
                            else:
                                dst = wraw[:, kc, :ncols]
                                srcv = win[:, kc, :]
                            s.dma_start(dst, srcv).then_inc(dX.h, 16); dX.n += 16
                ldc = dX.n
                is_wq = wdst is wqb
                with nc.Block() as b:
                    @b.vector
                    def _(v):
                        v.wait_ge(dX.h, ldc)
                        for kc in range(KC):
                            if is_wq:
                                v.tensor_scalar(wdst[:, kc, :], wraw[:, kc, :ncols],
                                                rca[kc][:, 0:1], 0.125,
                                                op0=OP.mult, op1=OP.mult)
                            else:
                                v.tensor_scalar(wdst[:, kc, :], wraw[:, kc, :ncols],
                                                rca[kc][:, 0:1], None, op0=OP.mult)

            if debug:
                d_mod = mk_dbg("modT", [128, 96], F32)
                d_rc = mk_dbg("rcols", [128, KC], F32)
                d_wqb = mk_dbg("wqb", [128, 512], BF16)
                d_wvb = mk_dbg("wvb", [128, 512], BF16)
                d_wob = mk_dbg("wob", [128, D], BF16)
                d_sct = mk_dbg("scb2", [128, 2, 8], BF16)
                with nc.Block() as b:
                    @b.gpsimd
                    def _(g):
                        g.dma_start(d_mod[:, :], modT[:, :]).then_inc(dY.h, 16); dY.n += 16
                        g.dma_start(d_rc[:, :], rcols[:, :]).then_inc(dY.h, 16); dY.n += 16
                        g.dma_start(d_wqb[:, :], wqb[:, 0, :]).then_inc(dY.h, 16); dY.n += 16
                        g.dma_start(d_wvb[:, :], wvb[:, 0, :]).then_inc(dY.h, 16); dY.n += 16
                        g.dma_start(d_wob[:, :], wob[:, 0, :]).then_inc(dY.h, 16); dY.n += 16
                        g.dma_start(d_sct[:, :, :], scb2[:, :, :]).then_inc(dY.h, 16); dY.n += 16

        with contextlib.ExitStack() as r2:
            fwin0 = SB(r2, "fwin0", [128, FF], F32)
            fwin1 = SB(r2, "fwin1", [128, FF], F32)
            fq_xn = SB(r2, "fq_xn", [128, FF], F32)
            fq_aa = SB(r2, "fq_aa", [128, FF], F32)
            fq_sg = SB(r2, "fq_sg", [128, FF], F32)
            fq_vv = SB(r2, "fq_vv", [128, FF], F32)
            fq_bm = SB(r2, "fq_bm", [128, 256], F32)
            fq_sc = SB(r2, "fq_sc", [128, 256], F32)
            fq_rs = SB(r2, "fq_rs", [128, 256], F32)
            fqo = SB(r2, "fqo", [128, FF], BF16)
            fevac = SB(r2, "fevac", [128, FF], BF16)
            ftp0 = PS(r2, "ftp0", [128, 1024], BF16)
            ftp1 = PS(r2, "ftp1", [128, 1024], BF16)
            fq = {"xn": fq_xn, "aa": fq_aa, "sg": fq_sg, "vv": fq_vv,
                  "bmax": fq_bm, "scale": fq_sc, "rs": fq_rs}
            fbufs = [fwin0, fwin1]
            ftps = [ftp0, ftp1]
            tiles = []
            wgatev = wgate_in.rearrange("(t p) d -> t p d", p=128)
            wupv = wup_in.rearrange("(t p) d -> t p d", p=128)
            wdownv = wdown_in.rearrange("(t p) d -> t p d", p=128)
            for t in range(FKC):
                tiles.append(("wg", t))
            for t in range(FKC):
                tiles.append(("wu", t))
            for t in range(KC):
                tiles.append(("wd", t))

            def tile_src(kind, t):
                return {"wg": wgatev, "wu": wupv, "wd": wdownv}[kind][t, :, :]

            def tile_n(kind):
                return D if kind in ("wg", "wu") else FF

            with nc.Block() as b:
                @b.sync
                def _(s):
                    k0, t0 = tiles[0]
                    s.dma_start(fbufs[0][:, :tile_n(k0)], tile_src(k0, t0)
                                ).then_inc(dX.h, 16); dX.n += 16

            for ti, (kind, t) in enumerate(tiles):
                n = tile_n(kind)
                nb = n // 16
                nch = n // 128
                ld_done = dX.n
                with nc.Block() as b:
                    @b.sync
                    def _(s):
                        if ti + 1 < len(tiles):
                            k1, t1 = tiles[ti + 1]
                            s.dma_start(fbufs[(ti + 1) % 2][:, :tile_n(k1)],
                                        tile_src(k1, t1)).then_inc(dX.h, 16); dX.n += 16

                    @b.vector
                    def _(v):
                        v.wait_ge(dX.h, ld_done)
                        emit_quant(v, fbufs[ti % 2][:, :n], fqo[:, :n], fq, n, nb)
                ngroups = (nch + 7) // 8
                store_done = dY.n
                for gi in range(ngroups):
                    c0 = gi * 8
                    c1 = min(c0 + 8, nch)
                    with nc.Block() as b:
                        @b.tensor
                        def _(tt):
                            for kc in range(c0, c1):
                                tt.transpose(ftps[gi % 2][:, (kc - c0) * 128:(kc - c0 + 1) * 128],
                                             fqo[:, kc * 128:(kc + 1) * 128],
                                             idn[:, :])
                    with nc.Block() as b:
                        @b.vector
                        def _(v):
                            v.wait_ge(dY.h, store_done)
                            v.tensor_copy(fevac[:, c0 * 128:c1 * 128],
                                          ftps[gi % 2][:, :(c1 - c0) * 128])
                with nc.Block() as b:
                    @b.gpsimd
                    def _(g):
                        dst = {"wg": wgq_d, "wu": wuq_d, "wd": wdq_d}[kind]
                        for kc in range(nch):
                            g.dma_start(dst[kc, :, t * 128:(t + 1) * 128],
                                        fevac[:, kc * 128:(kc + 1) * 128]
                                        ).then_inc(dY.h, 16); dY.n += 16
        ffnw_stored = dY.n

        with contextlib.ExitStack() as r3:
            xb0 = SB(r3, "xb0", [128, D], F32)
            xb1 = SB(r3, "xb1", [128, D], F32)
            hbuf0 = SB(r3, "hbuf0", [128, D], F32)
            hbuf1 = SB(r3, "hbuf1", [128, D], F32)
            tbuf0 = SB(r3, "tbuf0", [128, D], BF16)
            tbuf1 = SB(r3, "tbuf1", [128, D], BF16)
            tT0 = SB(r3, "tT0", [128, KC, 128], BF16)
            tT1 = SB(r3, "tT1", [128, KC, 128], BF16)
            sqscr = SB(r3, "sqscr", [128, D], BF16)
            ssum0 = SB(r3, "ssum0", [128, 1], F32)
            ssum1 = SB(r3, "ssum1", [128, 1], F32)
            ssq0 = SB(r3, "ssq0", [128, 1], F32)
            ssq1 = SB(r3, "ssq1", [128, 1], F32)
            mn0 = SB(r3, "mn0", [128, 1], F32)
            mn1 = SB(r3, "mn1", [128, 1], F32)
            vr0 = SB(r3, "vr0", [128, 1], F32)
            vr1 = SB(r3, "vr1", [128, 1], F32)
            rsv0 = SB(r3, "rsv0", [128, 1], F32)
            rsv1 = SB(r3, "rsv1", [128, 1], F32)
            rsq0 = SB(r3, "rsq0", [128, 1], F32)
            rsq1 = SB(r3, "rsq1", [128, 1], F32)
            gtmp = SB(r3, "gtmp", [128, 8], F32)
            qTs = SB(r3, "qTs", [128, 4, N], BF16)
            kTs = SB(r3, "kTs", [128, 4, N], BF16)
            vTs = SB(r3, "vTs", [128, NT, 512], BF16)
            gSig = SB(r3, "gSig", [128, NT, 8], F32)
            outTs = SB(r3, "outTs", [128, 4, N], BF16)
            Psb = SB(r3, "Psb", [128, N], BF16)
            ptpsb = SB(r3, "ptpsb", [128, N], BF16)
            rowmx = SB(r3, "rowmx", [128, 1], F32)
            nbias = SB(r3, "nbias", [128, 1], F32)
            rsum = SB(r3, "rsum", [128, 1], F32)
            rrs = SB(r3, "rrs", [128, 1], F32)
            gam = SB(r3, "gam", [128, 1], F32)
            diag = SB(r3, "diag", [128, 128], BF16)
            x1b0 = SB(r3, "x1b0", [128, D], F32)
            x1b1 = SB(r3, "x1b1", [128, D], F32)
            d_stats_sb = SB(r3, "d_stats_sb", [128, 8], F32)
            xbs = [xb0, xb1]
            hbufs = [hbuf0, hbuf1]
            tbufs = [tbuf0, tbuf1]
            tTs = [tT0, tT1]
            ssums = [ssum0, ssum1]
            ssqs = [ssq0, ssq1]
            mns = [mn0, mn1]
            vrs = [vr0, vr1]
            rsvs = [rsv0, rsv1]
            rsqs = [rsq0, rsq1]
            x1bs = [x1b0, x1b1]

            for seq in range(2):
                tok0 = seq * N

                with nc.Block() as b:
                    @b.gpsimd
                    def _(g):
                        g.wait_ge(dY.h, mod_stored)
                        g.dma_start(SAr[:, :], modv[0 * 2 + seq:0 * 2 + seq + 1, :]
                                    .broadcast_to([128, D])).then_inc(dW.h, 16); dW.n += 16
                        g.dma_start(CA1[:, :], modv[1 * 2 + seq:1 * 2 + seq + 1, :]
                                    .broadcast_to([128, D])).then_inc(dW.h, 16); dW.n += 16
                bc_done = dW.n

                with contextlib.ExitStack() as r4:
                    tps0 = PS(r4, "tps0_%d" % seq, [128, 1024], BF16)
                    tps1 = PS(r4, "tps1_%d" % seq, [128, 1024], BF16)
                    qps = PS(r4, "qps_%d" % seq, [128, 512], F32)
                    kps = PS(r4, "kps_%d" % seq, [128, 512], F32)
                    vps = PS(r4, "vps_%d" % seq, [128, 512], F32)
                    gps = PS(r4, "gps_%d" % seq, [128, 8], F32)
                    tpss = [tps0, tps1]
                    X0, V0, A0, T0 = dX.n, sV.n, sA.n, sT.n
                    KV, KA, KT = 5, 3, 2

                    def Vc(i, s):
                        return V0 + KV * i + s

                    def Ac(i, s):
                        return A0 + KA * i + s

                    def Tc(i, s):
                        return T0 + KT * i + s

                    with nc.Block() as b:
                        @b.sync
                        def _(s):
                            for i in range(NT):
                                if i >= 2:
                                    s.wait_ge(sV.h, Vc(i - 2, 2))
                                ins = s.dma_start(xbs[i % 2][:, :],
                                                  xv[tok0 + i * 128: tok0 + (i + 1) * 128, :])
                                dX.inc(ins, 16)

                        @b.tensor
                        def _(t):
                            for i in range(NT):
                                t.wait_ge(sV.h, Vc(i, 3))
                                if i >= 2:
                                    t.wait_ge(sV.h, Vc(i - 2, 4))
                                for kc in range(KC):
                                    t.transpose(
                                        tpss[i % 2][:, kc * 128:(kc + 1) * 128],
                                        tbufs[i % 2][:, kc * 128:(kc + 1) * 128],
                                        idn[:, :])
                                fence(t, sT)
                                t.wait_ge(sV.h, Vc(i, 4))
                                if i >= 1:
                                    t.wait_ge(sV.h, Vc(i - 1, 5))
                                for pr in range(4):
                                    for kc in range(KC):
                                        t.matmul(qps[:, pr * 128:(pr + 1) * 128],
                                                 wqb[:, kc, pr * 128:(pr + 1) * 128],
                                                 tTs[i % 2][:, kc, :],
                                                 start=(kc == 0), stop=(kc == KC - 1))
                                    for kc in range(KC):
                                        t.matmul(kps[:, pr * 128:(pr + 1) * 128],
                                                 wkb[:, kc, pr * 128:(pr + 1) * 128],
                                                 tTs[i % 2][:, kc, :],
                                                 start=(kc == 0), stop=(kc == KC - 1))
                                for kc in range(KC):
                                    t.matmul(vps[:, :], tTs[i % 2][:, kc, :],
                                             wvb[:, kc, :],
                                             start=(kc == 0), stop=(kc == KC - 1))
                                for kc in range(KC):
                                    t.matmul(gps[:, :], tTs[i % 2][:, kc, :],
                                             wgb[:, kc, :],
                                             start=(kc == 0), stop=(kc == KC - 1))
                                fence(t, sT)

                        @b.vector
                        def _(v):
                            for i in range(NT):
                                xb = xbs[i % 2]
                                hb = hbufs[i % 2]
                                v.wait_ge(dX.h, X0 + 16 * (i + 1))
                                if i == 0:
                                    v.wait_ge(dW.h, bc_done)
                                v.reduce_sum(ssums[i % 2][:, :], xb[:, :], axis=AX.X)
                                v.scalar_tensor_tensor(
                                    sqscr[:, :], xb[:, :], 1.0, xb[:, :],
                                    op0=OP.mult, op1=OP.mult,
                                    accum_out=ssqs[i % 2][:, :])
                                fence(v, sV)
                                v.wait_ge(sA.h, Ac(i, 1))
                                v.tensor_scalar(hb[:, :], xb[:, :], mns[i % 2][:, 0:1],
                                                rsvs[i % 2][:, 0:1],
                                                op0=OP.subtract, op1=OP.mult)
                                v.tensor_tensor(hb[:, :], hb[:, :], CA1[:, :], op=OP.mult)
                                v.tensor_tensor(hb[:, :], hb[:, :], SAr[:, :], op=OP.add)
                                v.scalar_tensor_tensor(
                                    sqscr[:, :], hb[:, :], 1.0, hb[:, :],
                                    op0=OP.mult, op1=OP.mult,
                                    accum_out=ssqs[i % 2][:, :])
                                fence(v, sV)
                                v.wait_ge(sA.h, Ac(i, 2))
                                v.tensor_scalar(tbufs[i % 2][:, :], hb[:, :],
                                                rsqs[i % 2][:, 0:1], None, op0=OP.mult)
                                fence(v, sV)
                                v.wait_ge(sT.h, Tc(i, 1))
                                v.tensor_copy(
                                    tTs[i % 2][:, :, :].rearrange("p c n -> p (c n)"),
                                    tpss[i % 2][:, :])
                                fence(v, sV)
                                v.wait_ge(sT.h, Tc(i, 2))
                                v.wait_ge(sA.h, Ac(i, 3))
                                for pr in range(4):
                                    v.tensor_copy(qTs[:, pr, i * 128:(i + 1) * 128],
                                                  qps[:, pr * 128:(pr + 1) * 128])
                                    v.tensor_copy(kTs[:, pr, i * 128:(i + 1) * 128],
                                                  kps[:, pr * 128:(pr + 1) * 128])
                                v.tensor_scalar(gtmp[:, :], gtmp[:, :], 1.0, None, op0=OP.add)
                                v.tensor_copy(vTs[:, i, :], vps[:, :])
                                v.reciprocal(gSig[:, i, :], gtmp[:, :])
                                fence(v, sV)

                        @b.scalar
                        def _(a):
                            for i in range(NT):
                                a.wait_ge(sV.h, Vc(i, 1))
                                a.activation(mns[i % 2][:, :], ssums[i % 2][:, :],
                                             ACT.Copy, scale=c1d[:, 0:1])
                                a.activation(am2[:, :], ssums[i % 2][:, :],
                                             ACT.Square, scale=c1d[:, 0:1])
                                a.drain()
                                a.activation(anm2[:, :], am2[:, :],
                                             ACT.Copy, scale=negone[:, 0:1],
                                             bias=1e-5)
                                a.drain()
                                a.activation(avrp[:, :], ssqs[i % 2][:, :],
                                             ACT.Ln, scale=c1d[:, 0:1],
                                             bias=anm2[:, 0:1])
                                a.drain()
                                a.activation(rsvs[i % 2][:, :], avrp[:, :],
                                             ACT.Exp, scale=nhalf[:, 0:1])
                                fence(a, sA)
                                a.wait_ge(sV.h, Vc(i, 2))
                                a.activation(rsqs[i % 2][:, :], ssqs[i % 2][:, :],
                                             ACT.Ln, scale=c1d[:, 0:1],
                                             bias=epsrms[:, 0:1])
                                a.drain()
                                a.activation(rsqs[i % 2][:, :], rsqs[i % 2][:, :],
                                             ACT.Exp, scale=nhalf[:, 0:1])
                                fence(a, sA)
                                a.wait_ge(sT.h, Tc(i, 2))
                                a.activation(gtmp[:, :], gps[:, :], ACT.Exp,
                                             scale=negone[:, 0:1])
                                fence(a, sA)

                if debug and seq == 0:
                    d_ca = mk_dbg("CA1", [128, D], F32)
                    d_t0 = mk_dbg("tb0", [128, D], BF16)
                    d_h0 = mk_dbg("h0", [128, D], F32)
                    d_qt = mk_dbg("qT", [128, 4, 256], BF16)
                    d_vt = mk_dbg("vT", [128, 512], BF16)
                    d_gs = mk_dbg("gS", [128, 8], F32)
                    with nc.Block() as b:
                        @b.gpsimd
                        def _(g):
                            g.dma_start(d_ca[:, :], CA1[:, :]).then_inc(dY.h, 16); dY.n += 16
                            g.dma_start(d_t0[:, :], tbufs[0][:, :]).then_inc(dY.h, 16); dY.n += 16
                            g.dma_start(d_h0[:, :], hbufs[0][:, :]).then_inc(dY.h, 16); dY.n += 16
                            g.dma_start(d_qt[:, :, :], qTs[:, :, 0:256]).then_inc(dY.h, 16); dY.n += 16
                            g.dma_start(d_vt[:, :], vTs[:, 0, :]).then_inc(dY.h, 16); dY.n += 16
                            g.dma_start(d_gs[:, :], gSig[:, 0, :]).then_inc(dY.h, 16); dY.n += 16

                with contextlib.ExitStack() as r5:
                    Sps = PS(r5, "Sps_%d" % seq, [128, N], F32)
                    oTps = PS(r5, "oTps_%d" % seq, [64, 128], F32)
                    V0, A0, T0 = sV.n, sA.n, sT.n
                    KV, KA, KT = 4, 1, 3

                    def Vc(k, s):
                        return V0 + KV * k + s

                    def Ac(k, s):
                        return A0 + KA * k + s

                    def Tc(k, s):
                        return T0 + KT * k + s

                    iters = [(i, h) for i in range(NT) for h in range(H)]

                    with nc.Block() as b:
                        @b.tensor
                        def _(t):
                            for k, (i, h) in enumerate(iters):
                                nkb = i + 1
                                pr, par = h // 2, h % 2
                                tp = (64 * par, 0)
                                if k >= 1:
                                    t.wait_ge(sV.h, Vc(k - 1, 3))
                                for kb in range(nkb):
                                    ins = t.matmul(
                                        Sps[:, kb * 128:(kb + 1) * 128],
                                        qTs[64 * par:64 * par + 64, pr, i * 128:(i + 1) * 128],
                                        kTs[64 * par:64 * par + 64, pr, kb * 128:(kb + 1) * 128],
                                        start=True, stop=True, tile_position=tp)
                                sT.inc(ins)
                                t.wait_ge(sV.h, Vc(k, 2))
                                for kb in range(nkb):
                                    ins = t.matmul(
                                        Sps[:, kb * 128:(kb + 1) * 128],
                                        Psb[:, kb * 128:(kb + 1) * 128],
                                        diag[:, :], start=True, stop=True)
                                sT.inc(ins)
                                t.wait_ge(sV.h, Vc(k, 3))
                                if k >= 1:
                                    t.wait_ge(sV.h, Vc(k - 1, 4))
                                for kb in range(nkb):
                                    ins = t.matmul(
                                        oTps[:, :],
                                        vTs[:, kb, h * 64:(h + 1) * 64],
                                        ptpsb[:, kb * 128:(kb + 1) * 128],
                                        start=(kb == 0), stop=(kb == nkb - 1))
                                sT.inc(ins)

                        @b.vector
                        def _(v):
                            for k, (i, h) in enumerate(iters):
                                nkb = i + 1
                                v.wait_ge(sT.h, Tc(k, 1))
                                v.tensor_tensor(Sps[:, i * 128:(i + 1) * 128],
                                                Sps[:, i * 128:(i + 1) * 128],
                                                msk[:, :], op=OP.add)
                                v.reduce_max(nbias[:, :], Sps[:, :nkb * 128],
                                             axis=AX.X, negate=True)
                                fence(v, sV)
                                v.wait_ge(sA.h, Ac(k, 1))
                                if k >= 1:
                                    v.wait_ge(sT.h, Tc(k - 1, 2))
                                v.reciprocal(rrs[:, :], rsum[:, :])
                                v.tensor_tensor(diag[:, :], idn[:, :],
                                                gSig[:, i, h:h + 1]
                                                .broadcast_to([128, 128]), op=OP.mult)
                                v.tensor_tensor(diag[:, :], diag[:, :],
                                                rrs[:, 0:1]
                                                .broadcast_to([128, 128]),
                                                op=OP.mult)
                                fence(v, sV)
                                v.wait_ge(sT.h, Tc(k, 2))
                                ins = v.tensor_copy(ptpsb[:, :nkb * 128],
                                                    Sps[:, :nkb * 128])
                                sV.inc(ins)
                                v.wait_ge(sT.h, Tc(k, 3))
                                ins = v.tensor_copy(
                                    outTs[64 * (h % 2):64 * (h % 2) + 64, h // 2,
                                          i * 128:(i + 1) * 128],
                                    oTps[:, :])
                                sV.inc(ins)

                        @b.scalar
                        def _(a):
                            for k, (i, h) in enumerate(iters):
                                nkb = i + 1
                                a.wait_ge(sV.h, Vc(k, 1))
                                if k >= 1:
                                    a.wait_ge(sT.h, Tc(k - 1, 2))
                                    a.wait_ge(sV.h, Vc(k - 1, 2))
                                a.activation(Psb[:, :nkb * 128],
                                             Sps[:, :nkb * 128],
                                             ACT.Exp, bias=nbias[:, 0:1],
                                             accum_out=rsum[:, :])
                                fence(a, sA)

                with contextlib.ExitStack() as r6:
                    pps0 = PS(r6, "pps0_%d" % seq, [128, 512], F32)
                    pps1 = PS(r6, "pps1_%d" % seq, [128, 512], F32)
                    ppss = [pps0, pps1]
                    with nc.Block() as b:
                        @b.gpsimd
                        def _(g):
                            g.wait_ge(dY.h, mod_stored)
                            g.dma_start(GAr[:, :], modv[2 * 2 + seq:2 * 2 + seq + 1, :]
                                        .broadcast_to([128, D])).then_inc(dW.h, 16); dW.n += 16
                    ga_done = dW.n

                    X0, V0, A0, T0, G0 = dX.n, sV.n, sA.n, sT.n, dY.n
                    KV, KA, KT = 1, 1, 1

                    def Vc(i, s):
                        return V0 + i + s

                    def Ac(i, s):
                        return A0 + i + s

                    def Tc(i, s):
                        return T0 + i + s

                    with nc.Block() as b:
                        @b.sync
                        def _(s):
                            for i in range(NT):
                                if i >= 2:
                                    s.wait_ge(sV.h, Vc(i - 2, 1))
                                ins = s.dma_start(xbs[i % 2][:, :],
                                                  xv[tok0 + i * 128: tok0 + (i + 1) * 128, :])
                                dX.inc(ins, 16)

                        @b.tensor
                        def _(t):
                            for i in range(NT):
                                if i >= 1:
                                    t.wait_ge(sV.h, Vc(i - 1, 1))
                                for dh in range(2):
                                    for kc in range(4):
                                        ins = t.matmul(
                                            ppss[dh][:, :],
                                            outTs[:, kc, i * 128:(i + 1) * 128],
                                            wob[:, kc, dh * 512:(dh + 1) * 512],
                                            start=(kc == 0), stop=(kc == 3))
                                sT.inc(ins)

                        @b.vector
                        def _(v):
                            for i in range(NT):
                                gt = seq * NT + i
                                v.wait_ge(sT.h, Tc(i, 1))
                                v.wait_ge(dX.h, X0 + 16 * (i + 1))
                                if i == 0:
                                    v.wait_ge(dW.h, ga_done)
                                if i >= 2:
                                    v.wait_ge(dY.h, G0 + 16 * (i - 1))
                                    v.wait_ge(sA.h, Ac(i - 2, 1))
                                x1b = x1bs[i % 2]
                                for dh in range(2):
                                    sl = slice(dh * 512, (dh + 1) * 512)
                                    v.tensor_tensor(x1b[:, sl], GAr[:, sl],
                                                    ppss[dh][:, :], op=OP.mult)
                                v.tensor_tensor(x1b[:, :], x1b[:, :],
                                                xbs[i % 2][:, :], op=OP.add)
                                v.reduce_sum(ssums[i % 2][:, :], x1b[:, :], axis=AX.X)
                                v.scalar_tensor_tensor(
                                    sqscr[:, :], x1b[:, :], 1.0, x1b[:, :],
                                    op0=OP.mult, op1=OP.mult,
                                    accum_out=ssqs[i % 2][:, :])
                                fence(v, sV)

                        @b.scalar
                        def _(a):
                            for i in range(NT):
                                gt = seq * NT + i
                                a.wait_ge(sV.h, Vc(i, 1))
                                a.activation(mn_tiles[gt][:, :], ssums[i % 2][:, :],
                                             ACT.Copy, scale=c1d[:, 0:1])
                                a.activation(am2[:, :], ssums[i % 2][:, :],
                                             ACT.Square, scale=c1d[:, 0:1])
                                a.drain()
                                a.activation(anm2[:, :], am2[:, :],
                                             ACT.Copy, scale=negone[:, 0:1],
                                             bias=1e-5)
                                a.drain()
                                a.activation(avrp[:, :], ssqs[i % 2][:, :],
                                             ACT.Ln, scale=c1d[:, 0:1],
                                             bias=anm2[:, 0:1])
                                a.drain()
                                a.activation(rsv_tiles[gt][:, :], avrp[:, :],
                                             ACT.Exp, scale=nhalf[:, 0:1])
                                fence(a, sA)

                        @b.gpsimd
                        def _(g):
                            for i in range(NT):
                                g.wait_ge(sA.h, Ac(i, 1))
                                ins = g.dma_start(
                                    x1d[tok0 + i * 128: tok0 + (i + 1) * 128, :],
                                    x1bs[i % 2][:, :])
                                dY.inc(ins, 16)

        x1_stored = dY.n
        attn_es.close()

        NGS = [(0, 512), (512, 512), (1024, 512), (1536, 512), (2048, 512), (2560, 256)]
        with contextlib.ExitStack() as r7:
            wgq_sb = SB(r7, "wgq_sb", [128, KC, FF], BF16)
            wuq_sb = SB(r7, "wuq_sb", [128, KC, FF], BF16)
            wdq_sb = SB(r7, "wdq_sb", [128, FKC, D], BF16)
            x1t0 = SB(r7, "x1t0", [128, D], F32)
            x1t1 = SB(r7, "x1t1", [128, D], F32)
            h2b = SB(r7, "h2b", [128, D], F32)
            h2q = SB(r7, "h2q", [128, D], BF16)
            h2qT = SB(r7, "h2qT", [128, KC, 128], BF16)
            sg0 = SB(r7, "sg0", [128, 512], F32)
            sg1 = SB(r7, "sg1", [128, 512], F32)
            hfb = SB(r7, "hfb", [128, 512], F32)
            hfq = SB(r7, "hfq", [128, FF], BF16)
            hfqT = SB(r7, "hfqT", [128, FKC, 128], BF16)
            outb = SB(r7, "outb", [128, D], F32)
            q_xn = SB(r7, "q_xn", [128, D], F32)
            q_aa = SB(r7, "q_aa", [128, D], F32)
            q_sg = SB(r7, "q_sg", [128, D], F32)
            q_vv = SB(r7, "q_vv", [128, D], F32)
            q_bm = SB(r7, "q_bm", [128, 64], F32)
            q_sc = SB(r7, "q_sc", [128, 64], F32)
            q_rs = SB(r7, "q_rs", [128, 64], F32)
            t2ps = PS(r7, "t2ps", [128, 1024], BF16)
            gatep = PS(r7, "gatep", [128, 512], F32)
            upp = PS(r7, "upp", [128, 512], F32)
            hftp0 = PS(r7, "hftp0", [128, 1024], BF16)
            hftp1 = PS(r7, "hftp1", [128, 1024], BF16)
            dnp0 = PS(r7, "dnp0", [128, 512], F32)
            dnp1 = PS(r7, "dnp1", [128, 512], F32)
            q2 = {"xn": q_xn, "aa": q_aa, "sg": q_sg, "vv": q_vv,
                  "bmax": q_bm, "scale": q_sc, "rs": q_rs}
            x1ts = [x1t0, x1t1]
            sgs = [sg0, sg1]
            hftps = [hftp0, hftp1]
            dnps = [dnp0, dnp1]

            with nc.Block() as b:
                @b.sync
                def _(s):
                    s.wait_ge(dY.h, ffnw_stored)
                    for kc in range(KC):
                        s.dma_start(wgq_sb[:, kc, :], wgq_d[kc, :, :]).then_inc(dX.h, 16); dX.n += 16
                        s.dma_start(wuq_sb[:, kc, :], wuq_d[kc, :, :]).then_inc(dX.h, 16); dX.n += 16
                    for kc in range(FKC):
                        s.dma_start(wdq_sb[:, kc, :], wdq_d[kc, :, :]).then_inc(dX.h, 16); dX.n += 16

                @b.gpsimd
                def _(g):
                    g.dma_start(SBr[:, :], modv[3 * 2 + 0:3 * 2 + 1, :]
                                .broadcast_to([128, D])).then_inc(dW.h, 16); dW.n += 16
                    g.dma_start(CB1[:, :], modv[4 * 2 + 0:4 * 2 + 1, :]
                                .broadcast_to([128, D])).then_inc(dW.h, 16); dW.n += 16
                    g.dma_start(GBr[:, :], modv[5 * 2 + 0:5 * 2 + 1, :]
                                .broadcast_to([128, D])).then_inc(dW.h, 16); dW.n += 16
            w_done = dX.n
            bc0_done = dW.n

            X0, V0, A0, T0, G0 = dX.n, sV.n, sA.n, sT.n, dY.n
            KV, KA, KT, NG = 12, 6, 11, 6

            def Vc(i, s):
                return V0 + KV * i + s

            def Ac(i, s):
                return A0 + KA * i + s

            def Tc(i, s):
                return T0 + KT * i + s

            TGRP = [(0, 8), (8, 16), (16, 22)]

            with nc.Block() as b:
                @b.scalar
                def _(a):
                    a.add_instruction(mybir.InstLoadActFuncSet(
                        name=nc.get_next_instruction_name(),
                        act_func_set_id=18, ins=[], outs=[]))

            with nc.Block() as b:
                @b.sync
                def _(s):
                    s.wait_ge(dY.h, x1_stored)
                    for i in range(32):
                        if i >= 2:
                            s.wait_ge(sV.h, Vc(i - 2, 12))
                        ins = s.dma_start(x1ts[i % 2][:, :],
                                          x1d[i * 128:(i + 1) * 128, :])
                        dX.inc(ins, 16)

                @b.gpsimd
                def _(g):
                    for i in range(16):
                        g.wait_ge(sV.h, Vc(i, 12))
                        ins = g.dma_start(yv[i * 128:(i + 1) * 128, :], outb[:, :])
                        dY.inc(ins, 16)
                    g.dma_start(SBr[:, :], modv[3 * 2 + 1:3 * 2 + 2, :]
                                .broadcast_to([128, D])).then_inc(dW.h, 16); dW.n += 16
                    g.dma_start(CB1[:, :], modv[4 * 2 + 1:4 * 2 + 2, :]
                                .broadcast_to([128, D])).then_inc(dW.h, 16); dW.n += 16
                    g.dma_start(GBr[:, :], modv[5 * 2 + 1:5 * 2 + 2, :]
                                .broadcast_to([128, D])).then_inc(dW.h, 16); dW.n += 16
                    for i in range(16, 32):
                        g.wait_ge(sV.h, Vc(i, 12))
                        ins = g.dma_start(yv[i * 128:(i + 1) * 128, :], outb[:, :])
                        dY.inc(ins, 16)
                    g.wait_ge(dY.h, dY.n)

                bc1_done = dW.n

                @b.vector
                def _(v):
                    for i in range(32):
                        v.wait_ge(dX.h, X0 + 16 * (i + 1))
                        if i == 0:
                            v.wait_ge(dW.h, bc0_done)
                        if i == 16:
                            v.wait_ge(dW.h, bc1_done)
                        if i >= 1:
                            v.wait_ge(sT.h, Tc(i - 1, 7))
                        v.tensor_scalar(h2b[:, :], x1ts[i % 2][:, :],
                                        mn_tiles[i][:, 0:1], rsv_tiles[i][:, 0:1],
                                        op0=OP.subtract, op1=OP.mult)
                        v.tensor_tensor(h2b[:, :], h2b[:, :], CB1[:, :], op=OP.mult)
                        v.tensor_tensor(h2b[:, :], h2b[:, :], SBr[:, :], op=OP.add)
                        emit_quant(v, h2b[:, :], h2q[:, :], q2, D, D // 16)
                        ins = v.engine_nop()
                        sV.inc(ins)
                        v.wait_ge(sT.h, Tc(i, 1))
                        ins = v.tensor_copy(
                            h2qT[:, :, :].rearrange("p c n -> p (c n)")[:, :1024],
                            t2ps[:, :])
                        sV.inc(ins)
                        for ngi, (c0, w) in enumerate(NGS):
                            v.wait_ge(sA.h, Ac(i, 1 + ngi))
                            v.wait_ge(sT.h, Tc(i, 2 + ngi))
                            if i >= 1:
                                v.wait_ge(sT.h, Tc(i - 1, 8 + ngi // 2))
                            v.tensor_tensor(hfb[:, :w], sgs[ngi % 2][:, :w],
                                            upp[:, :w], op=OP.mult)
                            emit_quant(v, hfb[:, :w], hfq[:, c0:c0 + w], q2, w, w // 16)
                            ins = v.engine_nop()
                            sV.inc(ins)
                        for gi, (g0, g1) in enumerate(TGRP):
                            v.wait_ge(sT.h, Tc(i, 8 + gi))
                            ins = v.tensor_copy(
                                hfqT[:, :, :].rearrange("p c n -> p (c n)")
                                [:, g0 * 128:g1 * 128],
                                hftps[gi % 2][:, :(g1 - g0) * 128])
                            sV.inc(ins)
                        v.wait_ge(sT.h, Tc(i, 11))
                        if i >= 1:
                            v.wait_ge(dY.h, G0 + 16 * i)
                        for dh in range(2):
                            sl = slice(dh * 512, (dh + 1) * 512)
                            v.tensor_tensor(outb[:, sl], GBr[:, sl],
                                            dnps[dh][:, :], op=OP.mult)
                        ins = v.tensor_tensor(outb[:, :], outb[:, :],
                                              x1ts[i % 2][:, :], op=OP.add)
                        sV.inc(ins)

                @b.tensor
                def _(t):
                    for i in range(32):
                        t.wait_ge(sV.h, Vc(i, 1))
                        if i >= 1:
                            t.wait_ge(sV.h, Vc(i - 1, 2))
                        for kc in range(KC):
                            ins = t.transpose(t2ps[:, kc * 128:(kc + 1) * 128],
                                              h2q[:, kc * 128:(kc + 1) * 128],
                                              idn[:, :])
                        sT.inc(ins)
                        if i == 0:
                            t.wait_ge(dX.h, w_done)
                        t.wait_ge(sV.h, Vc(i, 2))
                        for ngi, (c0, w) in enumerate(NGS):
                            if i >= 1 or ngi >= 1:
                                pi, png = (i, ngi - 1) if ngi >= 1 else (i - 1, 5)
                                t.wait_ge(sA.h, Ac(pi, 1 + png))
                                t.wait_ge(sV.h, Vc(pi, 3 + png))
                            for kc in range(KC):
                                t.matmul(gatep[:, :w], h2qT[:, kc, :],
                                         wgq_sb[:, kc, c0:c0 + w],
                                         start=(kc == 0), stop=(kc == KC - 1))
                            for kc in range(KC):
                                ins = t.matmul(upp[:, :w], h2qT[:, kc, :],
                                               wuq_sb[:, kc, c0:c0 + w],
                                               start=(kc == 0), stop=(kc == KC - 1))
                            sT.inc(ins)
                        for gi, (g0, g1) in enumerate(TGRP):
                            t.wait_ge(sV.h, Vc(i, 3 + (2 * gi + 1)))
                            if gi == 2:
                                t.wait_ge(sV.h, Vc(i, 9))
                            elif i >= 1:
                                t.wait_ge(sV.h, Vc(i - 1, 11 if gi == 0 else 10))
                            for kc in range(g0, g1):
                                ins = t.transpose(
                                    hftps[gi % 2][:, (kc - g0) * 128:(kc - g0 + 1) * 128],
                                    hfq[:, kc * 128:(kc + 1) * 128],
                                    idn[:, :])
                            sT.inc(ins)
                        t.wait_ge(sV.h, Vc(i, 11))
                        if i >= 1:
                            t.wait_ge(sV.h, Vc(i - 1, 12))
                        for dh in range(2):
                            for kc in range(FKC):
                                ins = t.matmul(dnps[dh][:, :], hfqT[:, kc, :],
                                               wdq_sb[:, kc, dh * 512:(dh + 1) * 512],
                                               start=(kc == 0), stop=(kc == FKC - 1))
                        sT.inc(ins)

                @b.scalar
                def _(a):
                    for i in range(32):
                        for ngi, (c0, w) in enumerate(NGS):
                            a.wait_ge(sT.h, Tc(i, 2 + ngi))
                            if i >= 1 or ngi >= 2:
                                pi, png = (i, ngi - 2) if ngi >= 2 else (i - 1, 4 + ngi)
                                a.wait_ge(sV.h, Vc(pi, 3 + png))
                            ins = a.activation(sgs[ngi % 2][:, :w], gatep[:, :w],
                                               ACT.Silu)
                            sA.inc(ins)

    return nc


_NC_CACHE = None


def kernel(L0_attn_tokens, L0_attn_wq, L0_attn_wk, L0_attn_wv, L0_attn_wo,
           L0_attn_w_lr, L0_attn_w_target, L0_attn_w_gates, L0_attn_rms_w,
           L0_ffn_w_gate, L0_ffn_w_up, L0_ffn_w_down, L0_c, L0_wmod):
    global _NC_CACHE
    if _NC_CACHE is None:
        _NC_CACHE = build_program()
    nc = _NC_CACHE

    x = np.ascontiguousarray(L0_attn_tokens, dtype=np.float32)
    iden = np.eye(128, dtype=np.float32).astype(BF)
    cmask = np.where(np.triu(np.ones((128, 128), dtype=bool), 1),
                     np.float32(NEG_BIG), np.float32(0.0))
    shared = {
        "wq": np.ascontiguousarray(L0_attn_wq, dtype=np.float32),
        "wk": np.ascontiguousarray(L0_attn_wk, dtype=np.float32),
        "wv": np.ascontiguousarray(L0_attn_wv, dtype=np.float32),
        "wo": np.ascontiguousarray(L0_attn_wo, dtype=np.float32),
        "wgates": np.ascontiguousarray(L0_attn_w_gates, dtype=np.float32),
        "rmsw": np.ascontiguousarray(L0_attn_rms_w, dtype=np.float32),
        "wmod": np.ascontiguousarray(L0_wmod, dtype=np.float32),
        "wgate": np.ascontiguousarray(L0_ffn_w_gate, dtype=np.float32),
        "wup": np.ascontiguousarray(L0_ffn_w_up, dtype=np.float32),
        "wdown": np.ascontiguousarray(L0_ffn_w_down, dtype=np.float32),
        "iden": iden,
        "cmask": cmask,
        "cvals": np.stack([
            np.full(128, 1e-5, np.float32),
            np.full(128, -0.5, np.float32),
            np.full(128, -1.0, np.float32),
            np.full(128, 0.125, np.float32),
            np.full(128, 1.0 / D, np.float32),
            np.full(128, EPS_RMS, np.float32),
            np.full(128, -0.25, np.float32),
            np.zeros(128, np.float32),
        ], axis=1),
    }
    cfull = np.ascontiguousarray(L0_c, dtype=np.float32)
    in_maps = []
    for core in range(8):
        m = dict(shared)
        m["x"] = np.ascontiguousarray(x[2 * core:2 * core + 2])
        m["cvec"] = np.ascontiguousarray(cfull[2 * core:2 * core + 2])
        in_maps.append(m)

    res = run_bass_kernel_spmd(nc, in_maps, core_ids=list(range(8)))
    out = np.concatenate([np.asarray(r["y"], dtype=np.float32)
                          for r in res.results], axis=0)
    return out
